# version 10
# baseline (speedup 1.0000x reference)
import numpy as np
import ml_dtypes
import concourse.bass as bass
import concourse.mybir as mybir
from concourse.bass_utils import run_bass_kernel_spmd

BF = ml_dtypes.bfloat16
F32 = mybir.dt.float32
BF16 = mybir.dt.bfloat16
AF = mybir.ActivationFunctionType
ALU = mybir.AluOpType
AX = mybir.AxisListType

D = 1024
S = 2048
NB = 32
NCORES = 8
SEQ_PER_CORE = NB // NCORES
T = 512
NT = 8
NQT = 4
CHUNKS_PER_SEQ = S // T
H = 16
EPS = 1e-5
NEG = -1.0e30
SLOTS = 6
WSLOTS = 3
NSLAB = 96
SLAB_E = 4096

SL_QK0, SL_V0, SL_PROJ0, SL_W10, SL_W20, SL_H10, SL_H20, SL_W30 = 0, 4, 6, 8, 16, 24, 56, 88


def _pack_lhsT(WT, Kt, m_per_slab):
    K, M = WT.shape
    Mt = M // 128
    arr = WT.reshape(Kt, 128, Mt, 128)
    ns = Mt // m_per_slab
    arr = arr.reshape(Kt, 128, ns, m_per_slab, 128)
    return np.ascontiguousarray(arr.transpose(2, 1, 0, 3, 4))


def _host_pack(inputs):
    wqkv = inputs["L0_attn_W_qkv"]
    wqk_T = np.ascontiguousarray(wqkv[:2048].T)
    wv_T = np.ascontiguousarray(wqkv[2048:].T)
    wout_T = np.ascontiguousarray(inputs["L0_attn_W_out"].T)
    w1_T = np.ascontiguousarray(inputs["L0_ffn_w1_w"].T)
    w2_T = np.ascontiguousarray(inputs["L0_ffn_w2_w"].T)
    h1_T = np.ascontiguousarray(inputs["L0_ffn_h1_w"].T)
    h2_T = np.ascontiguousarray(inputs["L0_ffn_h2_w"].T)
    w3_T = np.ascontiguousarray(inputs["L0_ffn_w3_w"].T)

    slabs = np.empty((NSLAB, 128, SLAB_E), dtype=np.float32)
    slabs[SL_QK0:SL_QK0 + 4] = _pack_lhsT(wqk_T, 8, 4).reshape(4, 128, SLAB_E)
    vv = wv_T.reshape(8, 128, 1024)
    slabs[SL_V0] = vv[:, :, :512].transpose(1, 0, 2).reshape(128, SLAB_E)
    slabs[SL_V0 + 1] = vv[:, :, 512:].transpose(1, 0, 2).reshape(128, SLAB_E)
    slabs[SL_PROJ0:SL_PROJ0 + 2] = _pack_lhsT(wout_T, 8, 4).reshape(2, 128, SLAB_E)
    slabs[SL_W10:SL_W10 + 8] = _pack_lhsT(w1_T, 8, 4).reshape(8, 128, SLAB_E)
    slabs[SL_W20:SL_W20 + 8] = _pack_lhsT(w2_T, 8, 4).reshape(8, 128, SLAB_E)
    slabs[SL_H10:SL_H10 + 32] = _pack_lhsT(h1_T, 32, 1).reshape(32, 128, SLAB_E)
    slabs[SL_H20:SL_H20 + 32] = _pack_lhsT(h2_T, 32, 1).reshape(32, 128, SLAB_E)
    slabs[SL_W30:SL_W30 + 8] = _pack_lhsT(w3_T, 32, 1).reshape(8, 128, SLAB_E)

    ii = np.arange(128)[:, None]
    masks = np.full((128, 768), NEG, dtype=np.float32)
    j = np.arange(384)[None, :]
    masks[:, 0:384] = np.where((j >= ii) & (j <= ii + 256), 0.0, NEG)
    j = np.arange(256)[None, :]
    masks[:, 384:640] = np.where(j <= ii + 128, 0.0, NEG)
    j = np.arange(128)[None, :]
    masks[:, 640:768] = np.where(j <= ii, 0.0, NEG)

    ident = np.eye(128, dtype=np.float32)

    def col128(v):
        return np.ascontiguousarray(v.reshape(-1, 128).T)

    scales = np.concatenate([
        col128(inputs["L0_ffn_w1_s"]), col128(inputs["L0_ffn_w2_s"]),
        col128(inputs["L0_ffn_h1_s"]), col128(inputs["L0_ffn_h2_s"]),
        col128(inputs["L0_ffn_w3_s"]), col128(inputs["L0_attn_gamma"]),
        col128(inputs["L0_attn_beta"]),
    ], axis=1).astype(np.float32)

    return slabs, masks, ident, scales


class Tok:
    __slots__ = ("sem", "val", "eng")
    def __init__(self, sem, val, eng):
        self.sem, self.val, self.eng = sem, val, eng


class Plan:
    ENGINES = ("sync", "gpsimd", "tensor", "vector", "scalar")

    def __init__(self, sems):
        self.ops = {e: [] for e in self.ENGINES}
        self.sems = sems
        self.cnt = {name: 0 for name in sems}
        self.hwm = {e: {name: 0 for name in sems} for e in self.ENGINES}

    def wait(self, eng, tok):
        if tok is None or tok.eng == eng:
            return
        if self.hwm[eng][tok.sem] >= tok.val:
            return
        self.hwm[eng][tok.sem] = tok.val
        h, v = self.sems[tok.sem], tok.val
        self.ops[eng].append(lambda e: e.wait_ge(h, v))

    def waits(self, eng, toks):
        for t in toks:
            self.wait(eng, t)

    def op(self, eng, fn, inc=None, k=1):
        if inc is None:
            self.ops[eng].append(fn)
            return None
        self.cnt[inc] += k
        h = self.sems[inc]
        self.ops[eng].append(lambda e: fn(e).then_inc(h, k))
        return Tok(inc, self.cnt[inc], eng)

    def raw(self, eng, fn):
        self.ops[eng].append(fn)


DBG = False


def build_nc(nseq=SEQ_PER_CORE):
    NCH = nseq * CHUNKS_PER_SEQ
    nc = bass.Bass("TRN2", target_bir_lowering=False)
    x_in = nc.dram_tensor("x", [nseq, D, S], F32, kind="ExternalInput").ap()
    w_in = nc.dram_tensor("w", [NSLAB, 128, SLAB_E], F32, kind="ExternalInput").ap()
    mk_in = nc.dram_tensor("mk", [128, 768], F32, kind="ExternalInput").ap()
    id_in = nc.dram_tensor("idin", [128, 128], F32, kind="ExternalInput").ap()
    sc_in = nc.dram_tensor("sc", [128, 152], F32, kind="ExternalInput").ap()
    out_t = nc.dram_tensor("out", [nseq, D, S], F32, kind="ExternalOutput").ap()
    wscr = nc.dram_tensor("wscr", [NSLAB, 128, SLAB_E], BF16, kind="Internal").ap()

    import contextlib
    ctx = contextlib.ExitStack()
    sb = lambda name, shape, dt: ctx.enter_context(nc.sbuf_tensor(name, shape, dt))
    ps = lambda name, shape, dt: ctx.enter_context(nc.psum_tensor(name, shape, dt))

    xb = sb("xb", [128, NT, T], F32)
    x2b = sb("x2b", [128, NT, T], BF16)
    abo = sb("abo", [128, NT, T], F32)
    scr = sb("scr", [128, 2, T], F32)
    xn = sb("xn", [128, 2, NT, T], BF16)
    qb = sb("qb", [128, NT, T], BF16)
    kring = sb("kring", [128, NT, SLOTS, 128], BF16)
    vring = sb("vring", [128, SLOTS, D], BF16)
    attn = abo.bitcast(BF16)
    attnT = sb("attnT", [128, 3, 128], BF16)
    aout = sb("aout", [128, NT, T], BF16)
    hbuf = sb("hbuf", [128, 32, T], BF16)
    hpbuf = sb("hpbuf", [128, 32, T], BF16)
    wring = sb("wring", [128, WSLOTS, SLAB_E], BF16)
    msk = sb("msk", [128, 768], BF16)
    idm = sb("idm", [128, 128], BF16)
    scsb = sb("scsb", [128, 152], F32)
    ones_c = sb("ones_c", [128, 1], F32)
    ones_b = sb("ones_b", [128, 1], BF16)
    ones_r = sb("ones_r", [1, 128], F32)
    rows = sb("rows", [1, 6, T], F32)
    nmax = sb("nmax", [128, 1], F32)
    esum = sb("esum", [128, 1], F32)
    rsum = sb("rsum", [128, 8], F32)

    R_M, R_VAR, R_T, R_R, R_R1, R_MR1 = 0, 1, 2, 3, 4, 5

    pm = [ps("pm0", [128, 512], F32), ps("pm1", [128, 512], F32)]
    pst = ps("pst", [128, 512], F32)
    pbc = [ps("pbc0", [128, 512], F32), ps("pbc1", [128, 512], F32)]
    psc = ps("psc", [128, 512], F32)
    ptrb = ps("ptrb", [128, 3, 128], BF16)
    ppv = ps("ppv", [128, 128], F32)

    sem_names = ["sx", "sw", "swf", "spro", "spe", "sve", "sse", "sst", "sgp"]
    sems = {n: ctx.enter_context(nc.semaphore(n)) for n in sem_names}
    P = Plan(sems)

    c_s1 = lambda m: scsb[:, m:m + 1]
    c_s2 = lambda m: scsb[:, 32 + m:33 + m]
    c_sh1 = lambda m: scsb[:, 64 + m:65 + m]
    c_sh2 = lambda m: scsb[:, 96 + m:97 + m]
    c_s3 = lambda m: scsb[:, 128 + m:129 + m]
    c_gam = lambda t: scsb[:, 136 + t:137 + t]
    c_bet = lambda t: scsb[:, 144 + t:145 + t]

    t_sc = P.op("sync", lambda e: e.dma_start(out=scsb[:, :], in_=sc_in[:, :]), inc="sx", k=16)
    P.op("gpsimd", lambda e: e.dma_start(out=msk[:, :], in_=mk_in[:, :]), inc="spro", k=16)
    t_consts = P.op("gpsimd", lambda e: e.dma_start(out=idm[:, :], in_=id_in[:, :]),
                    inc="spro", k=16)
    slab_cast = []
    for s_ in range(NSLAB):
        tok = P.op("gpsimd", lambda e, s_=s_: e.dma_start(out=wscr[s_], in_=w_in[s_]),
                   inc="spro", k=16)
        slab_cast.append(tok)
    P.op("gpsimd", lambda e: e.memset(ones_c[:, :], 1.0), inc="sgp")
    P.op("gpsimd", lambda e: e.memset(ones_b[:, :], 1.0), inc="sgp")
    t_ones = P.op("gpsimd", lambda e: e.memset(ones_r[:, :], 1.0), inc="sgp")

    slab_ready = {}
    slab_freed = {}
    gs_counter = [0]

    def load_slab(slab_idx):
        gs = gs_counter[0]
        gs_counter[0] += 1
        slot = gs % WSLOTS
        if gs >= WSLOTS:
            P.wait("sync", slab_freed[gs - WSLOTS])
        if gs < NSLAB:
            P.wait("sync", slab_cast[slab_idx])
        tok = P.op("sync", lambda e, slot=slot, si=slab_idx: e.dma_start(
            out=wring[:, slot, :], in_=wscr[si]), inc="sw", k=16)
        slab_ready[gs] = (tok, slot)
        return gs

    slab_gs = {}

    def slab(epoch, i):
        key = (epoch, i)
        if key not in slab_gs:
            slab_gs[key] = load_slab(i)
        tok, slot = slab_ready[slab_gs[key]]
        return tok, slot

    def slab_done(epoch, i, tok):
        slab_freed[slab_gs[(epoch, i)]] = tok

    def w_qk(slot, k, m):
        base = (k * 4 + m) * 128
        return wring[:, slot, base:base + 128]

    def w_v(slot, k):
        return wring[:, slot, k * 512:(k + 1) * 512]

    def w_h(slot, k):
        return wring[:, slot, k * 128:(k + 1) * 128]

    def row(i):
        return rows[:, i, :]

    st = {
        "sq_free": [None, None],
        "pm_free": [None, None],
        "pm_next": 0,
        "pbc_free": [None, None],
        "pst_free": None,
        "psc_free": None,
        "attn_slot_free": [None] * 8,
        "ptr_free": None,
        "ppv_free": None,
        "attnT_free": None,
        "last_bcast": [],
        "abo_store": None,
        "abo_mid_free": None,
        "x_free": None,
        "x2_free": None,
        "hp_free": None,
        "h_free": None,
        "xn_free": [None, None],
        "q_free": None,
        "kv_free": None,
        "front": {},
    }

    def next_pm():
        b = st["pm_next"]
        st["pm_next"] ^= 1
        return b

    def ln_stats(src, src_toks, bf=False):
        ones = ones_b if bf else ones_c
        first = True
        mm_toks = []
        for t in range(NT):
            P.waits("vector", src_toks)
            if t >= 2:
                P.wait("vector", mm_toks[t - 2])
            elif st["sq_free"][t % 2] is not None:
                P.wait("vector", st["sq_free"][t % 2])
            tsq = P.op("vector", lambda e, t=t, src=src: e.tensor_mul(
                scr[:, t % 2, :], src[:, t, :], src[:, t, :]), inc="sve")
            P.wait("tensor", tsq)
            if first:
                P.wait("tensor", st["pst_free"])
                P.wait("tensor", t_ones)
                first = False
            tok = P.op("tensor", lambda e, t=t: e.matmul(
                pst[32:33, :], ones_c[:, :], scr[:, t % 2, :],
                start=(t == 0), stop=(t == NT - 1)), inc="spe")
            mm_toks.append(tok)
        for t in range(NT):
            P.waits("tensor", src_toks)
            tok = P.op("tensor", lambda e, t=t, ones=ones, src=src: e.matmul(
                pst[0:1, :], ones[:, :], src[:, t, :],
                start=(t == 0), stop=(t == NT - 1)), inc="spe")
        st["sq_free"] = [mm_toks[NT - 2], mm_toks[NT - 1]]
        return tok

    def ln_rows(stats_tok, full=False):
        P.waits("vector", st["last_bcast"])
        P.waits("scalar", st["last_bcast"])
        P.wait("vector", stats_tok)
        P.op("vector", lambda e: e.tensor_scalar_mul(row(R_M), pst[0:1, :], 1.0 / D))
        P.op("vector", lambda e: e.tensor_scalar_mul(row(R_T), pst[32:33, :], 1.0 / D))
        P.op("vector", lambda e: e.tensor_mul(row(R_VAR), row(R_M), row(R_M)))
        tv = P.op("vector", lambda e: e.tensor_sub(row(R_VAR), row(R_T), row(R_VAR)),
                  inc="sve")
        st["pst_free"] = tv
        P.wait("scalar", tv)
        if not full:
            P.op("scalar", lambda e: e.activation(row(R_T), row(R_VAR), AF.Ln,
                                                  bias=np.float32(EPS)))
            t_r = P.op("scalar", lambda e: e.activation(row(R_T), row(R_T), AF.Exp,
                                                        scale=np.float32(-0.5)), inc="sse")
            P.wait("vector", t_r)
            t_mr = P.op("vector", lambda e: e.tensor_mul(row(R_R), row(R_M), row(R_T)),
                        inc="sve")
            return t_mr
        P.op("scalar", lambda e: e.activation(row(R_T), row(R_VAR), AF.Ln,
                                              bias=np.float32(EPS)))
        t_r1 = P.op("scalar", lambda e: e.activation(row(R_R1), row(R_T), AF.Exp,
                                                     scale=np.float32(-0.5)), inc="sse")
        P.wait("vector", t_r1)
        t_vpe = P.op("vector", lambda e: e.tensor_scalar_add(row(R_T), row(R_VAR),
                                                             float(EPS)), inc="sve")
        P.op("vector", lambda e: e.reciprocal(row(R_T), row(R_T)))
        tvr = P.op("vector", lambda e: e.tensor_mul(row(R_T), row(R_VAR), row(R_T)),
                   inc="sve")
        P.wait("scalar", tvr)
        P.op("scalar", lambda e: e.activation(row(R_T), row(R_T), AF.Ln,
                                              bias=np.float32(EPS)))
        t_r2 = P.op("scalar", lambda e: e.activation(row(R_T), row(R_T), AF.Exp,
                                                     scale=np.float32(-0.5)), inc="sse")
        P.wait("vector", t_r2)
        P.op("vector", lambda e: e.tensor_mul(row(R_MR1), row(R_M), row(R_R1)))
        P.op("vector", lambda e: e.tensor_mul(row(R_R), row(R_R1), row(R_T)))
        t_all = P.op("vector", lambda e: e.tensor_mul(row(R_T), row(R_M), row(R_R)),
                     inc="sve")
        return t_all

    def bcast(bank, r_idx, dep_tok):
        P.wait("tensor", dep_tok)
        P.wait("tensor", st["pbc_free"][bank])
        tok = P.op("tensor", lambda e, bank=bank, r_idx=r_idx: e.matmul(
            pbc[bank][:, :], ones_r[:, :], row(r_idx), start=True, stop=True), inc="spe")
        st["last_bcast"].append(tok)
        st["last_bcast"] = st["last_bcast"][-4:]
        return tok

    def front(c):
        b, p = divmod(c, CHUNKS_PER_SEQ)
        par = c % 2
        if c == 0:
            for eng in ("tensor", "vector", "scalar"):
                P.wait(eng, slab_cast[NSLAB - 1])
            P.wait("scalar", t_sc)
            P.wait("vector", t_sc)
        P.wait("sync", st["x_free"])
        x_ready = P.op("sync", lambda e, b=b, p=p: e.dma_start(
            out=xb[:, :, :], in_=x_in[b, :, p * T:(p + 1) * T].rearrange(
                "(nt q) t -> q nt t", q=128)), inc="sx", k=16)
        yield
        stats_tok = ln_stats(xb, [x_ready])
        yield
        t_all = ln_rows(stats_tok, full=True)
        yield
        t_RB = bcast(0, R_R, t_all)
        t_MB = bcast(1, R_T, t_all)
        xn_toks = []
        for t in range(NT):
            P.waits("vector", [t_RB, t_MB, x_ready])
            P.wait("vector", st["sq_free"][t % 2])
            P.wait("vector", st["xn_free"][par])
            P.op("vector", lambda e, t=t: e.tensor_mul(scr[:, t % 2, :], xb[:, t, :],
                                                       pbc[0][:, :]))
            P.op("vector", lambda e, t=t: e.tensor_sub(scr[:, t % 2, :],
                                                       scr[:, t % 2, :], pbc[1][:, :]))
            tok = P.op("vector", lambda e, t=t, par=par: e.tensor_scalar(
                xn[:, par, t, :], scr[:, t % 2, :], c_gam(t), c_bet(t),
                op0=ALU.mult, op1=ALU.add), inc="sve")
            st["sq_free"][t % 2] = tok
            xn_toks.append(tok)
            if t % 2 == 1:
                yield
        st["pbc_free"] = [xn_toks[NT - 1], xn_toks[NT - 1]]
        q_toks, k_toks, v_toks = [], [], []
        for m in range(16):
            w_tok, slot = slab(c, SL_QK0 + m // 4)
            bank = next_pm()
            P.wait("tensor", w_tok)
            P.wait("tensor", st["pm_free"][bank])
            P.waits("tensor", xn_toks)
            for k in range(NT):
                tok = P.op("tensor", lambda e, m=m, k=k, slot=slot, bank=bank, par=par:
                           e.matmul(
                    pm[bank][:, :], w_qk(slot, k, m % 4), xn[:, par, k, :],
                    start=(k == 0), stop=(k == NT - 1)),
                    inc="spe" if k == NT - 1 else None)
            if m % 4 == 3:
                slab_done(c, SL_QK0 + m // 4, tok)
            if m < 8:
                P.wait("scalar", tok)
                P.wait("scalar", st["q_free"])
                ev = P.op("scalar", lambda e, m=m, bank=bank: e.activation(
                    qb[:, m, :], pm[bank][:, :], AF.Copy, scale=np.float32(0.125)),
                    inc="sse")
                q_toks.append(ev)
            else:
                dt = m - 8
                P.wait("vector", tok)
                P.wait("vector", st["kv_free"])
                s0 = (4 * p) % SLOTS
                run1 = min(SLOTS - s0, 4)
                ev = P.op("vector", lambda e, dt=dt, s0=s0, run1=run1, bank=bank:
                          e.tensor_copy(
                    kring[:, dt, s0:s0 + run1, :], pm[bank][:, 0:run1 * 128]), inc="sve")
                if run1 < 4:
                    ev = P.op("vector", lambda e, dt=dt, run1=run1, bank=bank:
                              e.tensor_copy(
                        kring[:, dt, 0:4 - run1, :], pm[bank][:, run1 * 128:512]),
                        inc="sve")
                k_toks.append(ev)
            st["pm_free"][bank] = q_toks[-1] if m < 8 else k_toks[-1]
            if m % 4 == 3:
                yield
        for tt in range(NQT):
            for half in range(2):
                w_tok, slot = slab(c, SL_V0 + half)
                bank = next_pm()
                P.wait("tensor", w_tok)
                P.wait("tensor", st["pm_free"][bank])
                for k in range(NT):
                    tok = P.op("tensor", lambda e, tt=tt, k=k, slot=slot, bank=bank,
                               par=par: e.matmul(
                        pm[bank][:, :], xn[:, par, k, tt * 128:(tt + 1) * 128],
                        w_v(slot, k), start=(k == 0), stop=(k == NT - 1)),
                        inc="spe" if k == NT - 1 else None)
                if tt == NQT - 1:
                    slab_done(c, SL_V0 + half, tok)
                P.wait("vector", tok)
                P.wait("vector", st["kv_free"])
                slot_v = (4 * p + tt) % SLOTS
                ev = P.op("vector", lambda e, slot_v=slot_v, half=half, bank=bank:
                          e.tensor_copy(
                    vring[:, slot_v, half * 512:(half + 1) * 512], pm[bank][:, :]),
                    inc="sve")
                v_toks.append(ev)
                st["pm_free"][bank] = ev
        yield
        kv_wr = k_toks + v_toks
        pair_ev = []
        pv_state = {"last": None}

        def emitA(t, hg, jt_list, runs, W, moff):
            nm_toks = {}
            for h in range(hg, hg + 4):
                sl8 = (t * H + h) % 8
                sl4 = h % 4
                bankp = [psc, pst, pbc[0], pbc[1]][sl4]
                P.waits("tensor", kv_wr + q_toks)
                if sl4 == 0:
                    P.wait("tensor", st["psc_free"])
                elif sl4 == 1:
                    P.wait("tensor", st["pst_free"])
                else:
                    P.wait("tensor", st["pbc_free"][sl4 - 2])
                P.wait("tensor", t_consts)
                col = 0
                first = True
                for sl_, ln_ in runs:
                    P.op("tensor", lambda e, h=h, t=t, sl_=sl_, ln_=ln_, col=col,
                         first=first, bankp=bankp: e.matmul(
                        bankp[:, col:col + ln_ * 128],
                        qb[64 * (h % 2):64 * (h % 2) + 64, h // 2,
                           t * 128:(t + 1) * 128],
                        kring[64 * (h % 2):64 * (h % 2) + 64, h // 2,
                              sl_:sl_ + ln_, :],
                        start=first, stop=False))
                    col += ln_ * 128
                    first = False
                t_scm = P.op("tensor", lambda e, W=W, moff=moff, bankp=bankp: e.matmul(
                    bankp[:, 0:W], idm[:, :], msk[:, moff:moff + W],
                    start=False, stop=True), inc="spe")
                P.wait("vector", t_scm)
                t_mx = P.op("vector", lambda e, W=W, bankp=bankp: e.reduce_max(
                    nmax[:, :], bankp[:, 0:W], axis=AX.X, negate=True), inc="sve")
                P.wait("scalar", t_mx)
                P.wait("scalar", st["attn_slot_free"][sl8])
                P.wait("scalar", st["abo_mid_free"])
                t_ex = P.op("scalar", lambda e, W=W, sl8=sl8, bankp=bankp: e.activation(
                    attn[:, sl8, 0:W], bankp[:, 0:W], AF.Exp,
                    bias=nmax[:, :], scale=np.float32(1.0), accum_out=esum[:, :]),
                    inc="sse")
                if sl4 == 0:
                    st["psc_free"] = t_ex
                elif sl4 == 1:
                    st["pst_free"] = t_ex
                else:
                    st["pbc_free"][sl4 - 2] = t_ex
                P.wait("vector", t_ex)
                t_rc = P.op("vector", lambda e, sl8=sl8: e.reciprocal(
                    rsum[:, sl8:sl8 + 1], esum[:, :]), inc="sve")
                P.wait("scalar", t_rc)
                t_nm = P.op("scalar", lambda e, W=W, sl8=sl8: e.activation(
                    attn[:, sl8, 0:W], attn[:, sl8, 0:W], AF.Copy,
                    scale=rsum[:, sl8:sl8 + 1]), inc="sse")
                nm_toks[h] = t_nm
            return nm_toks

        def emitB(t, hg, jt_list, nj, nm_toks):
            for h in range(hg, hg + 4):
                sl8 = (t * H + h) % 8
                pp = h // 2
                P.wait("tensor", nm_toks[h])
                P.wait("tensor", st["ptr_free"])
                for bi in range(nj):
                    t_tr = P.op("tensor", lambda e, bi=bi, sl8=sl8: e.transpose(
                        ptrb[:, bi, :], attn[:, sl8, bi * 128:(bi + 1) * 128],
                        idm[:, :]), inc="spe" if bi == nj - 1 else None)
                st["attn_slot_free"][sl8] = t_tr
                P.wait("vector", t_tr)
                P.wait("vector", st["attnT_free"])
                t_cp = P.op("vector", lambda e, nj=nj: e.tensor_copy(
                    attnT[:, 0:nj, :], ptrb[:, 0:nj, :]), inc="sve")
                st["ptr_free"] = t_cp
                P.wait("tensor", t_cp)
                if h % 2 == 0:
                    P.wait("tensor", st["ppv_free"])
                for bi, jt in enumerate(jt_list):
                    t_pv = P.op("tensor", lambda e, h=h, bi=bi, jt=jt, nj=nj: e.matmul(
                        ppv[64 * (h % 2):64 * (h % 2) + 64, 0:128],
                        vring[:, jt % SLOTS, h * 64:(h + 1) * 64],
                        attnT[:, bi, :],
                        start=(bi == 0), stop=(bi == nj - 1)),
                        inc="spe" if (bi == nj - 1 and h % 2 == 1) else None)
                st["attnT_free"] = t_pv
                if h % 2 == 1:
                    pv_state["last"] = t_pv
                    P.wait("vector", t_pv)
                    ev = P.op("vector", lambda e, pp=pp, t=t: e.tensor_copy(
                        aout[:, pp, t * 128:(t + 1) * 128], ppv[:, 0:128]),
                        inc="sve")
                    pair_ev.append(ev)
                    st["ppv_free"] = ev

        pending = None
        for t in range(NQT):
            gt = 4 * p + t
            j0 = max(0, gt - 2)
            jt_list = list(range(j0, gt + 1))
            nj = len(jt_list)
            W = 128 * nj
            moff = {3: 0, 2: 384, 1: 640}[nj]
            runs = []
            for jt in jt_list:
                sl_ = jt % SLOTS
                if runs and runs[-1][0] + runs[-1][1] == sl_:
                    runs[-1][1] += 1
                else:
                    runs.append([sl_, 1])
            for hg in range(0, H, 4):
                nm_toks = emitA(t, hg, jt_list, runs, W, moff)
                yield
                if pending is not None:
                    pending()
                    yield
                pending = (lambda t=t, hg=hg, jl=jt_list, nj=nj, nm=nm_toks:
                           emitB(t, hg, jl, nj, nm))
        pending()
        pv_last = pv_state["last"]
        st["kv_free"] = pv_last
        st["q_free"] = pv_last
        st["front"][c] = {"pair_ev": pair_ev, "v_toks": v_toks, "rows_tok": t_all}

    def serial(c):
        par = c % 2
        fo = st["front"][c]
        pair_ev = fo["pair_ev"]
        t_r1B = bcast(0, R_R1, fo["rows_tok"])
        t_mrB = bcast(1, R_MR1, fo["rows_tok"])
        proj_toks = []
        for m in range(8):
            w_tok, slot = slab(c, SL_PROJ0 + m // 4)
            bank = next_pm()
            P.wait("tensor", w_tok)
            P.wait("tensor", st["pm_free"][bank])
            P.waits("tensor", pair_ev)
            for k in range(NT):
                tok = P.op("tensor", lambda e, m=m, k=k, slot=slot, bank=bank: e.matmul(
                    pm[bank][:, :], w_qk(slot, k, m % 4), aout[:, k, :],
                    start=(k == 0), stop=(k == NT - 1)),
                    inc="spe" if k == NT - 1 else None)
            if m % 4 == 3:
                slab_done(c, SL_PROJ0 + m // 4, tok)
            P.waits("vector", [tok, t_r1B, t_mrB])
            P.wait("vector", st["sq_free"][m % 2])
            P.wait("vector", st["abo_store"])
            P.op("vector", lambda e, m=m: e.tensor_mul(scr[:, m % 2, :], xb[:, m, :],
                                                       pbc[0][:, :]))
            P.op("vector", lambda e, m=m: e.tensor_sub(scr[:, m % 2, :],
                                                       scr[:, m % 2, :], pbc[1][:, :]))
            ev = P.op("vector", lambda e, m=m, bank=bank: e.tensor_add(
                abo[:, m, :], pm[bank][:, :], scr[:, m % 2, :]), inc="sve")
            st["sq_free"][m % 2] = ev
            proj_toks.append(ev)
            st["pm_free"][bank] = ev
        st["pbc_free"] = [proj_toks[-1], proj_toks[-1]]
        stats_tok = ln_stats(abo, proj_toks)
        t_mr = ln_rows(stats_tok, full=False)
        t_rB = bcast(0, R_T, t_mr)
        t_mrB = bcast(1, R_R, t_mr)
        x2_toks = []
        for t in range(NT):
            P.waits("vector", [t_rB, t_mrB])
            P.wait("vector", st["sq_free"][t % 2])
            P.wait("vector", st["x2_free"])
            P.op("vector", lambda e, t=t: e.tensor_mul(scr[:, t % 2, :], abo[:, t, :],
                                                       pbc[0][:, :]))
            P.op("vector", lambda e, t=t: e.tensor_sub(scr[:, t % 2, :],
                                                       scr[:, t % 2, :], pbc[1][:, :]))
            tok = P.op("vector", lambda e, t=t: e.tensor_add(
                x2b[:, t, :], xb[:, t, :], scr[:, t % 2, :]), inc="sve")
            st["sq_free"][t % 2] = tok
            x2_toks.append(tok)
        st["pbc_free"] = [x2_toks[-1], x2_toks[-1]]
        st["x_free"] = x2_toks[-1]
        st["abo_mid_free"] = x2_toks[-1]
        stats_tok = ln_stats(x2b, x2_toks, bf=True)
        t_mr = ln_rows(stats_tok, full=False)
        t_rB = bcast(0, R_T, t_mr)
        t_mrB = bcast(1, R_R, t_mr)
        xn2_toks = []
        for t in range(NT):
            P.waits("vector", [t_rB, t_mrB])
            P.waits("vector", fo["v_toks"])
            P.wait("vector", st["xn_free"][par])
            P.wait("vector", st["sq_free"][t % 2])
            P.op("vector", lambda e, t=t: e.tensor_mul(scr[:, t % 2, :], x2b[:, t, :],
                                                       pbc[0][:, :]))
            tok = P.op("vector", lambda e, t=t, par=par: e.tensor_sub(
                xn[:, par, t, :], scr[:, t % 2, :], pbc[1][:, :]), inc="sve")
            st["sq_free"][t % 2] = tok
            xn2_toks.append(tok)
        st["pbc_free"] = [xn2_toks[-1], xn2_toks[-1]]
        st["front"][c]["xn2_toks"] = xn2_toks

    def ffn(c, gen):
        par = c % 2
        fo = st["front"][c]
        xn2_toks = fo["xn2_toks"]

        def g_next(n=1):
            if gen is None:
                return
            for _ in range(n):
                try:
                    next(gen)
                except StopIteration:
                    return

        silu_toks = [None] * 32
        for m in range(32):
            w_tok, slot = slab(c, SL_W10 + m // 4)
            bank = next_pm()
            P.wait("tensor", w_tok)
            P.wait("tensor", st["pm_free"][bank])
            P.waits("tensor", xn2_toks)
            for k in range(NT):
                tok = P.op("tensor", lambda e, m=m, k=k, slot=slot, bank=bank, par=par:
                           e.matmul(
                    pm[bank][:, :], w_qk(slot, k, m % 4), xn[:, par, k, :],
                    start=(k == 0), stop=(k == NT - 1)),
                    inc="spe" if k == NT - 1 else None)
            if m % 4 == 3:
                slab_done(c, SL_W10 + m // 4, tok)
            P.wait("vector", tok)
            P.wait("vector", st["h_free"])
            ev = P.op("vector", lambda e, m=m, bank=bank: e.tensor_scalar_mul(
                hbuf[:, m, :], pm[bank][:, :], c_s1(m)), inc="sve")
            st["pm_free"][bank] = ev
            if m % 4 == 3:
                P.wait("scalar", ev)
                bt = P.op("scalar", lambda e, m=m: e.activation(
                    hbuf[:, m - 3:m + 1, :], hbuf[:, m - 3:m + 1, :], AF.Silu),
                    inc="sse")
                for mm_ in range(m - 3, m + 1):
                    silu_toks[mm_] = bt
            if m % 2 == 1:
                g_next()
        h_toks = [None] * 32
        for m in range(32):
            w_tok, slot = slab(c, SL_W20 + m // 4)
            bank = next_pm()
            P.wait("tensor", w_tok)
            P.wait("tensor", st["pm_free"][bank])
            for k in range(NT):
                tok = P.op("tensor", lambda e, m=m, k=k, slot=slot, bank=bank, par=par:
                           e.matmul(
                    pm[bank][:, :], w_qk(slot, k, m % 4), xn[:, par, k, :],
                    start=(k == 0), stop=(k == NT - 1)),
                    inc="spe" if k == NT - 1 else None)
            if m % 4 == 3:
                slab_done(c, SL_W20 + m // 4, tok)
            P.wait("vector", tok)
            P.wait("vector", silu_toks[m])
            ev = P.op("vector", lambda e, m=m, bank=bank: e.scalar_tensor_tensor(
                hbuf[:, m, :], pm[bank][:, :], c_s2(m), hbuf[:, m, :],
                op0=ALU.mult, op1=ALU.mult), inc="sve")
            h_toks[m] = ev
            st["pm_free"][bank] = ev
            if m % 2 == 1:
                g_next()
        st["xn_free"][par] = h_toks[31]

        def big(sl0, src, src_toks, dst, dst_free, scale_col):
            toks = [None] * 32
            for m in range(32):
                w_tok, slot = slab(c, sl0 + m)
                bank = next_pm()
                P.wait("tensor", w_tok)
                P.wait("tensor", st["pm_free"][bank])
                P.waits("tensor", src_toks)
                for k in range(32):
                    tok = P.op("tensor", lambda e, m=m, k=k, slot=slot, bank=bank,
                               src=src: e.matmul(
                        pm[bank][:, :], w_h(slot, k), src[:, k, :],
                        start=(k == 0), stop=(k == 31)),
                        inc="spe" if k == 31 else None)
                slab_done(c, sl0 + m, tok)
                P.wait("vector", tok)
                P.wait("vector", dst_free)
                ev = P.op("vector", lambda e, m=m, dst=dst, bank=bank: e.tensor_scalar_mul(
                    dst[:, m, :], pm[bank][:, :], scale_col(m)), inc="sve")
                st["pm_free"][bank] = ev
                if m % 4 == 3:
                    P.wait("scalar", ev)
                    bt = P.op("scalar", lambda e, m=m, dst=dst: e.activation(
                        dst[:, m - 3:m + 1, :], dst[:, m - 3:m + 1, :], AF.Gelu),
                        inc="sse")
                    for mm_ in range(m - 3, m + 1):
                        toks[mm_] = bt
                g_next(2)
            return toks

        hp_toks = big(SL_H10, hbuf, h_toks, hpbuf, st["hp_free"], c_sh1)
        hpp_toks = big(SL_H20, hpbuf, hp_toks, hbuf, None, c_sh2)
        st["hp_free"] = hpp_toks[31]
        if gen is not None:
            for _ in gen:
                pass
        f_toks = []
        for m in range(8):
            w_tok, slot = slab(c, SL_W30 + m)
            bank = next_pm()
            P.wait("tensor", w_tok)
            P.wait("tensor", st["pm_free"][bank])
            P.waits("tensor", hpp_toks)
            for k in range(32):
                tok = P.op("tensor", lambda e, m=m, k=k, slot=slot, bank=bank: e.matmul(
                    pm[bank][:, :], w_h(slot, k), hbuf[:, k, :],
                    start=(k == 0), stop=(k == 31)),
                    inc="spe" if k == 31 else None)
            slab_done(c, SL_W30 + m, tok)
            P.wait("scalar", tok)
            P.waits("scalar", [tk for tk in st["attn_slot_free"] if tk is not None])
            ev = P.op("scalar", lambda e, m=m, bank=bank: e.activation(
                abo[:, m, :], pm[bank][:, :], AF.Copy, scale=c_s3(m)), inc="sse")
            f_toks.append(ev)
            st["pm_free"][bank] = ev
            g_next()
        st["h_free"] = f_toks[-1]
        b, p = divmod(c, CHUNKS_PER_SEQ)
        stats_tok = ln_stats(abo, f_toks)
        t_mr = ln_rows(stats_tok, full=False)
        t_rB = bcast(0, R_T, t_mr)
        t_mrB = bcast(1, R_R, t_mr)
        out_toks = []
        for t in range(NT):
            P.waits("vector", [t_rB, t_mrB])
            P.wait("vector", st["sq_free"][t % 2])
            P.op("vector", lambda e, t=t: e.tensor_mul(scr[:, t % 2, :], abo[:, t, :],
                                                       pbc[0][:, :]))
            P.op("vector", lambda e, t=t: e.tensor_sub(scr[:, t % 2, :],
                                                       scr[:, t % 2, :], pbc[1][:, :]))
            tok = P.op("vector", lambda e, t=t: e.tensor_add(
                abo[:, t, :], x2b[:, t, :], scr[:, t % 2, :]), inc="sve")
            st["sq_free"][t % 2] = tok
            out_toks.append(tok)
        st["pbc_free"] = [out_toks[-1], out_toks[-1]]
        st["x2_free"] = out_toks[-1]
        P.waits("sync", out_toks)
        st["abo_store"] = P.op("sync", lambda e, b=b, p=p: e.dma_start(
            out=out_t[b, :, p * T:(p + 1) * T].rearrange("(nt q) t -> q nt t", q=128),
            in_=abo[:, :, :]), inc="sst", k=16)
        del st["front"][c]

    g = front(0)
    for _ in g:
        pass
    for c in range(NCH):
        serial(c)
        gen = front(c + 1) if c + 1 < NCH else None
        ffn(c, gen)

    final = P.cnt["sst"]
    P.raw("sync", lambda e: e.wait_ge(sems["sst"], final))

    with nc.Block() as block:
        @block.sync
        def _(sync):
            for fn in P.ops["sync"]:
                fn(sync)

        @block.gpsimd
        def _(gpsimd):
            for fn in P.ops["gpsimd"]:
                fn(gpsimd)

        @block.tensor
        def _(tensor):
            for fn in P.ops["tensor"]:
                fn(tensor)

        @block.vector
        def _(vector):
            for fn in P.ops["vector"]:
                fn(vector)

        @block.scalar
        def _(scalar):
            for fn in P.ops["scalar"]:
                fn(scalar)
    ctx.close()
    return nc


def kernel(**inputs):
    x = np.asarray(inputs["L0_attn_x"])
    slabs, masks, ident, scales = _host_pack({k: np.asarray(v) for k, v in inputs.items()})
    nc = build_nc()
    in_maps = []
    for core in range(NCORES):
        xc = x[core * SEQ_PER_CORE:(core + 1) * SEQ_PER_CORE]
        x_cm = np.ascontiguousarray(xc.transpose(0, 2, 1))
        in_maps.append({"x": x_cm, "w": slabs, "mk": masks, "idin": ident, "sc": scales})
    res = run_bass_kernel_spmd(nc, in_maps, core_ids=list(range(NCORES)))
    outs = []
    for core in range(NCORES):
        o = np.asarray(res.results[core]["out"])
        outs.append(np.ascontiguousarray(o.transpose(0, 2, 1)))
    return np.concatenate(outs, axis=0).astype(np.float32)


# version 12
# speedup vs baseline: 1.0974x; 1.0974x over previous
import numpy as np
import ml_dtypes
import concourse.bass as bass
import concourse.mybir as mybir
from concourse.bass_utils import run_bass_kernel_spmd

BF = ml_dtypes.bfloat16
F32 = mybir.dt.float32
BF16 = mybir.dt.bfloat16
AF = mybir.ActivationFunctionType
ALU = mybir.AluOpType
AX = mybir.AxisListType

D = 1024
S = 2048
NB = 32
NCORES = 8
SEQ_PER_CORE = NB // NCORES
T = 512
NT = 8
NQT = 4
CHUNKS_PER_SEQ = S // T
H = 16
EPS = 1e-5
NEG = -1.0e30
SLOTS = 6
WSLOTS = 3
NSLAB = 96
SLAB_E = 4096

SL_QK0, SL_V0, SL_PROJ0, SL_W10, SL_W20, SL_H10, SL_H20, SL_W30 = 0, 4, 6, 8, 16, 24, 56, 88


def _pack_lhsT(WT, Kt, m_per_slab):
    K, M = WT.shape
    Mt = M // 128
    arr = WT.reshape(Kt, 128, Mt, 128)
    ns = Mt // m_per_slab
    arr = arr.reshape(Kt, 128, ns, m_per_slab, 128)
    return np.ascontiguousarray(arr.transpose(2, 1, 0, 3, 4))


def _host_pack(inputs):
    wqkv = inputs["L0_attn_W_qkv"]
    wqk_T = np.ascontiguousarray(wqkv[:2048].T)
    wv_T = np.ascontiguousarray(wqkv[2048:].T)
    wout_T = np.ascontiguousarray(inputs["L0_attn_W_out"].T)
    w1_T = np.ascontiguousarray(inputs["L0_ffn_w1_w"].T)
    w2_T = np.ascontiguousarray(inputs["L0_ffn_w2_w"].T)
    h1_T = np.ascontiguousarray(inputs["L0_ffn_h1_w"].T)
    h2_T = np.ascontiguousarray(inputs["L0_ffn_h2_w"].T)
    w3_T = np.ascontiguousarray(inputs["L0_ffn_w3_w"].T)

    slabs = np.empty((NSLAB, 128, SLAB_E), dtype=np.float32)
    slabs[SL_QK0:SL_QK0 + 4] = _pack_lhsT(wqk_T, 8, 4).reshape(4, 128, SLAB_E)
    vv = wv_T.reshape(8, 128, 1024)
    slabs[SL_V0] = vv[:, :, :512].transpose(1, 0, 2).reshape(128, SLAB_E)
    slabs[SL_V0 + 1] = vv[:, :, 512:].transpose(1, 0, 2).reshape(128, SLAB_E)
    slabs[SL_PROJ0:SL_PROJ0 + 2] = _pack_lhsT(wout_T, 8, 4).reshape(2, 128, SLAB_E)
    slabs[SL_W10:SL_W10 + 8] = _pack_lhsT(w1_T, 8, 4).reshape(8, 128, SLAB_E)
    slabs[SL_W20:SL_W20 + 8] = _pack_lhsT(w2_T, 8, 4).reshape(8, 128, SLAB_E)
    slabs[SL_H10:SL_H10 + 32] = _pack_lhsT(h1_T, 32, 1).reshape(32, 128, SLAB_E)
    slabs[SL_H20:SL_H20 + 32] = _pack_lhsT(h2_T, 32, 1).reshape(32, 128, SLAB_E)
    slabs[SL_W30:SL_W30 + 8] = _pack_lhsT(w3_T, 32, 1).reshape(8, 128, SLAB_E)

    ii = np.arange(128)[:, None]
    masks = np.full((128, 768), NEG, dtype=np.float32)
    j = np.arange(384)[None, :]
    masks[:, 0:384] = np.where((j >= ii) & (j <= ii + 256), 0.0, NEG)
    j = np.arange(256)[None, :]
    masks[:, 384:640] = np.where(j <= ii + 128, 0.0, NEG)
    j = np.arange(128)[None, :]
    masks[:, 640:768] = np.where(j <= ii, 0.0, NEG)

    ident = np.eye(128, dtype=np.float32)

    def col128(v):
        return np.ascontiguousarray(v.reshape(-1, 128).T)

    scales = np.concatenate([
        col128(inputs["L0_ffn_w1_s"]), col128(inputs["L0_ffn_w2_s"]),
        col128(inputs["L0_ffn_h1_s"]), col128(inputs["L0_ffn_h2_s"]),
        col128(inputs["L0_ffn_w3_s"]), col128(inputs["L0_attn_gamma"]),
        col128(inputs["L0_attn_beta"]),
    ], axis=1).astype(np.float32)

    return slabs, masks, ident, scales


class Tok:
    __slots__ = ("sem", "val", "eng")
    def __init__(self, sem, val, eng):
        self.sem, self.val, self.eng = sem, val, eng


class Plan:
    ENGINES = ("sync", "gpsimd", "tensor", "vector", "scalar")

    def __init__(self, sems):
        self.ops = {e: [] for e in self.ENGINES}
        self.sems = sems
        self.cnt = {name: 0 for name in sems}
        self.hwm = {e: {name: 0 for name in sems} for e in self.ENGINES}

    def wait(self, eng, tok):
        if tok is None or tok.eng == eng:
            return
        if self.hwm[eng][tok.sem] >= tok.val:
            return
        self.hwm[eng][tok.sem] = tok.val
        h, v = self.sems[tok.sem], tok.val
        self.ops[eng].append(lambda e: e.wait_ge(h, v))

    def waits(self, eng, toks):
        for t in toks:
            self.wait(eng, t)

    def op(self, eng, fn, inc=None, k=1):
        if inc is None:
            self.ops[eng].append(fn)
            return None
        self.cnt[inc] += k
        h = self.sems[inc]
        self.ops[eng].append(lambda e: fn(e).then_inc(h, k))
        return Tok(inc, self.cnt[inc], eng)

    def raw(self, eng, fn):
        self.ops[eng].append(fn)


DBG = False


def build_nc(nseq=SEQ_PER_CORE):
    NCH = nseq * CHUNKS_PER_SEQ
    nc = bass.Bass("TRN2", target_bir_lowering=False)
    x_in = nc.dram_tensor("x", [nseq, D, S], F32, kind="ExternalInput").ap()
    w_in = nc.dram_tensor("w", [NSLAB, 128, SLAB_E], F32, kind="ExternalInput").ap()
    mk_in = nc.dram_tensor("mk", [128, 768], F32, kind="ExternalInput").ap()
    id_in = nc.dram_tensor("idin", [128, 128], F32, kind="ExternalInput").ap()
    sc_in = nc.dram_tensor("sc", [128, 152], F32, kind="ExternalInput").ap()
    out_t = nc.dram_tensor("out", [nseq, D, S], F32, kind="ExternalOutput").ap()
    wscr = nc.dram_tensor("wscr", [NSLAB, 128, SLAB_E], BF16, kind="Internal").ap()

    import contextlib
    ctx = contextlib.ExitStack()
    sb = lambda name, shape, dt: ctx.enter_context(nc.sbuf_tensor(name, shape, dt))
    ps = lambda name, shape, dt: ctx.enter_context(nc.psum_tensor(name, shape, dt))

    xb = sb("xb", [128, NT, T], F32)
    x2b = sb("x2b", [128, NT, T], BF16)
    abo = sb("abo", [128, NT, T], F32)
    scr = sb("scr", [128, 2, T], F32)
    sqb = sb("sqb", [128, 2, T], BF16)
    xn = sb("xn", [128, 2, NT, T], BF16)
    qb = sb("qb", [128, NT, T], BF16)
    kring = sb("kring", [128, NT, SLOTS, 128], BF16)
    vring = sb("vring", [128, SLOTS, D], BF16)
    attn = sb("attn", [128, 384], BF16)
    attnT = sb("attnT", [128, 3, 128], BF16)
    aout = sb("aout", [128, NT, T], BF16)
    hbuf = sb("hbuf", [128, 32, T], BF16)
    hpbuf = sb("hpbuf", [128, 32, T], BF16)
    wring = sb("wring", [128, WSLOTS, SLAB_E], BF16)
    msk = sb("msk", [128, 768], BF16)
    idm = sb("idm", [128, 128], BF16)
    scsb = sb("scsb", [128, 152], F32)
    ones_c = sb("ones_c", [128, 1], F32)
    ones_b = sb("ones_b", [128, 1], BF16)
    ones_r = sb("ones_r", [1, 128], F32)
    rows = sb("rows", [1, 6, T], F32)
    nmax = sb("nmax", [128, 1], F32)
    esum = sb("esum", [128, 1], F32)
    rsum = sb("rsum", [128, 1], F32)

    R_M, R_VAR, R_T, R_R, R_R1, R_MR1 = 0, 1, 2, 3, 4, 5

    pm = [ps("pm0", [128, 512], F32), ps("pm1", [128, 512], F32)]
    pst = ps("pst", [128, 512], F32)
    pbc = [ps("pbc0", [128, 512], F32), ps("pbc1", [128, 512], F32)]
    psc = ps("psc", [128, 512], F32)
    ptrb = ps("ptrb", [128, 3, 128], BF16)
    ppv = ps("ppv", [128, 128], F32)

    sem_names = ["sx", "sw", "swf", "spro", "spe", "sve", "sse", "sst", "sgp"]
    sems = {n: ctx.enter_context(nc.semaphore(n)) for n in sem_names}
    P = Plan(sems)

    c_s1 = lambda m: scsb[:, m:m + 1]
    c_s2 = lambda m: scsb[:, 32 + m:33 + m]
    c_sh1 = lambda m: scsb[:, 64 + m:65 + m]
    c_sh2 = lambda m: scsb[:, 96 + m:97 + m]
    c_s3 = lambda m: scsb[:, 128 + m:129 + m]
    c_gam = lambda t: scsb[:, 136 + t:137 + t]
    c_bet = lambda t: scsb[:, 144 + t:145 + t]

    t_sc = P.op("sync", lambda e: e.dma_start(out=scsb[:, :], in_=sc_in[:, :]), inc="sx", k=16)
    P.op("gpsimd", lambda e: e.dma_start(out=msk[:, :], in_=mk_in[:, :]), inc="spro", k=16)
    t_consts = P.op("gpsimd", lambda e: e.dma_start(out=idm[:, :], in_=id_in[:, :]),
                    inc="spro", k=16)
    slab_cast = []
    for s_ in range(NSLAB):
        tok = P.op("gpsimd", lambda e, s_=s_: e.dma_start(out=wscr[s_], in_=w_in[s_]),
                   inc="spro", k=16)
        slab_cast.append(tok)
    P.op("gpsimd", lambda e: e.memset(ones_c[:, :], 1.0), inc="sgp")
    P.op("gpsimd", lambda e: e.memset(ones_b[:, :], 1.0), inc="sgp")
    t_ones = P.op("gpsimd", lambda e: e.memset(ones_r[:, :], 1.0), inc="sgp")

    slab_ready = {}
    slab_freed = {}
    gs_counter = [0]

    def load_slab(slab_idx):
        gs = gs_counter[0]
        gs_counter[0] += 1
        slot = gs % WSLOTS
        if gs >= WSLOTS:
            P.wait("sync", slab_freed[gs - WSLOTS])
        if gs < NSLAB:
            P.wait("sync", slab_cast[slab_idx])
        tok = P.op("sync", lambda e, slot=slot, si=slab_idx: e.dma_start(
            out=wring[:, slot, :], in_=wscr[si]), inc="sw", k=16)
        slab_ready[gs] = (tok, slot)
        return gs

    slab_gs = {}

    def slab(epoch, i):
        key = (epoch, i)
        if key not in slab_gs:
            slab_gs[key] = load_slab(i)
        tok, slot = slab_ready[slab_gs[key]]
        return tok, slot

    def slab_done(epoch, i, tok):
        slab_freed[slab_gs[(epoch, i)]] = tok

    def w_qk(slot, k, m):
        base = (k * 4 + m) * 128
        return wring[:, slot, base:base + 128]

    def w_v(slot, k):
        return wring[:, slot, k * 512:(k + 1) * 512]

    def w_h(slot, k):
        return wring[:, slot, k * 128:(k + 1) * 128]

    def row(i):
        return rows[:, i, :]

    st = {
        "sq_free": [None, None],
        "pm_free": [None, None],
        "pm_next": 0,
        "pbc_free": [None, None],
        "pst_free": None,
        "psc_free": None,
        "ptr_free": None,
        "ppv_free": None,
        "attn_free": None,
        "attnT_free": None,
        "last_bcast": [],
        "abo_store": None,
        "x_free": None,
        "x2_free": None,
        "hp_free": None,
        "h_free": None,
        "xn_free": [None, None],
        "q_free": None,
        "kv_free": None,
        "front": {},
    }

    def next_pm():
        b = st["pm_next"]
        st["pm_next"] ^= 1
        return b

    def ln_stats(src, src_toks, bf=False):
        ones = ones_b if bf else ones_c
        first = True
        mm_toks = []
        for t in range(NT):
            P.waits("vector", src_toks)
            if t >= 2:
                P.wait("vector", mm_toks[t - 2])
            elif st["sq_free"][t % 2] is not None:
                P.wait("vector", st["sq_free"][t % 2])
            tsq = P.op("vector", lambda e, t=t, src=src: e.tensor_mul(
                sqb[:, t % 2, :], src[:, t, :], src[:, t, :]), inc="sve")
            P.wait("tensor", tsq)
            if first:
                P.wait("tensor", st["pst_free"])
                P.wait("tensor", t_ones)
                first = False
            tok = P.op("tensor", lambda e, t=t: e.matmul(
                pst[32:33, :], ones_b[:, :], sqb[:, t % 2, :],
                start=(t == 0), stop=(t == NT - 1)), inc="spe")
            mm_toks.append(tok)
        for t in range(NT):
            P.waits("tensor", src_toks)
            tok = P.op("tensor", lambda e, t=t, ones=ones, src=src: e.matmul(
                pst[0:1, :], ones[:, :], src[:, t, :],
                start=(t == 0), stop=(t == NT - 1)), inc="spe")
        st["sq_free"] = [mm_toks[NT - 2], mm_toks[NT - 1]]
        return tok

    def ln_rows(stats_tok, full=False):
        P.waits("vector", st["last_bcast"])
        P.waits("scalar", st["last_bcast"])
        P.wait("vector", stats_tok)
        P.op("vector", lambda e: e.tensor_scalar_mul(row(R_M), pst[0:1, :], 1.0 / D))
        P.op("vector", lambda e: e.tensor_scalar_mul(row(R_T), pst[32:33, :], 1.0 / D))
        P.op("vector", lambda e: e.tensor_mul(row(R_VAR), row(R_M), row(R_M)))
        tv = P.op("vector", lambda e: e.tensor_sub(row(R_VAR), row(R_T), row(R_VAR)),
                  inc="sve")
        st["pst_free"] = tv
        P.wait("scalar", tv)
        if not full:
            P.op("scalar", lambda e: e.activation(row(R_T), row(R_VAR), AF.Ln,
                                                  bias=np.float32(EPS)))
            t_r = P.op("scalar", lambda e: e.activation(row(R_T), row(R_T), AF.Exp,
                                                        scale=np.float32(-0.5)), inc="sse")
            P.wait("vector", t_r)
            t_mr = P.op("vector", lambda e: e.tensor_mul(row(R_R), row(R_M), row(R_T)),
                        inc="sve")
            return t_mr
        P.op("scalar", lambda e: e.activation(row(R_T), row(R_VAR), AF.Ln,
                                              bias=np.float32(EPS)))
        t_r1 = P.op("scalar", lambda e: e.activation(row(R_R1), row(R_T), AF.Exp,
                                                     scale=np.float32(-0.5)), inc="sse")
        P.wait("vector", t_r1)
        t_vpe = P.op("vector", lambda e: e.tensor_scalar_add(row(R_T), row(R_VAR),
                                                             float(EPS)), inc="sve")
        P.op("vector", lambda e: e.reciprocal(row(R_T), row(R_T)))
        tvr = P.op("vector", lambda e: e.tensor_mul(row(R_T), row(R_VAR), row(R_T)),
                   inc="sve")
        P.wait("scalar", tvr)
        P.op("scalar", lambda e: e.activation(row(R_T), row(R_T), AF.Ln,
                                              bias=np.float32(EPS)))
        t_r2 = P.op("scalar", lambda e: e.activation(row(R_T), row(R_T), AF.Exp,
                                                     scale=np.float32(-0.5)), inc="sse")
        P.wait("vector", t_r2)
        P.op("vector", lambda e: e.tensor_mul(row(R_MR1), row(R_M), row(R_R1)))
        P.op("vector", lambda e: e.tensor_mul(row(R_R), row(R_R1), row(R_T)))
        t_all = P.op("vector", lambda e: e.tensor_mul(row(R_T), row(R_M), row(R_R)),
                     inc="sve")
        return t_all

    def bcast(bank, r_idx, dep_tok):
        P.wait("tensor", dep_tok)
        P.wait("tensor", st["pbc_free"][bank])
        tok = P.op("tensor", lambda e, bank=bank, r_idx=r_idx: e.matmul(
            pbc[bank][:, :], ones_r[:, :], row(r_idx), start=True, stop=True), inc="spe")
        st["last_bcast"].append(tok)
        st["last_bcast"] = st["last_bcast"][-4:]
        return tok

    def front(c):
        b, p = divmod(c, CHUNKS_PER_SEQ)
        par = c % 2
        if c == 0:
            for eng in ("tensor", "vector", "scalar"):
                P.wait(eng, slab_cast[NSLAB - 1])
            P.wait("scalar", t_sc)
            P.wait("vector", t_sc)
        P.wait("sync", st["x_free"])
        x_ready = P.op("sync", lambda e, b=b, p=p: e.dma_start(
            out=xb[:, :, :], in_=x_in[b, :, p * T:(p + 1) * T].rearrange(
                "(nt q) t -> q nt t", q=128)), inc="sx", k=16)
        yield
        stats_tok = ln_stats(xb, [x_ready])
        yield
        t_all = ln_rows(stats_tok, full=True)
        yield
        t_RB = bcast(0, R_R, t_all)
        t_MB = bcast(1, R_T, t_all)
        xn_toks = []
        for t in range(NT):
            P.waits("vector", [t_RB, t_MB, x_ready])
            P.wait("vector", st["sq_free"][t % 2])
            P.wait("vector", st["xn_free"][par])
            P.op("vector", lambda e, t=t: e.tensor_mul(scr[:, t % 2, :], xb[:, t, :],
                                                       pbc[0][:, :]))
            P.op("vector", lambda e, t=t: e.tensor_sub(scr[:, t % 2, :],
                                                       scr[:, t % 2, :], pbc[1][:, :]))
            tok = P.op("vector", lambda e, t=t, par=par: e.tensor_scalar(
                xn[:, par, t, :], scr[:, t % 2, :], c_gam(t), c_bet(t),
                op0=ALU.mult, op1=ALU.add), inc="sve")
            st["sq_free"][t % 2] = tok
            xn_toks.append(tok)
            if t % 2 == 1:
                yield
        st["pbc_free"] = [xn_toks[NT - 1], xn_toks[NT - 1]]
        q_toks, k_toks, v_toks = [], [], []
        for m in range(16):
            w_tok, slot = slab(c, SL_QK0 + m // 4)
            bank = next_pm()
            P.wait("tensor", w_tok)
            P.wait("tensor", st["pm_free"][bank])
            P.waits("tensor", xn_toks)
            for k in range(NT):
                tok = P.op("tensor", lambda e, m=m, k=k, slot=slot, bank=bank, par=par:
                           e.matmul(
                    pm[bank][:, :], w_qk(slot, k, m % 4), xn[:, par, k, :],
                    start=(k == 0), stop=(k == NT - 1)),
                    inc="spe" if k == NT - 1 else None)
            if m % 4 == 3:
                slab_done(c, SL_QK0 + m // 4, tok)
            if m < 8:
                P.wait("scalar", tok)
                P.wait("scalar", st["q_free"])
                ev = P.op("scalar", lambda e, m=m, bank=bank: e.activation(
                    qb[:, m, :], pm[bank][:, :], AF.Copy, scale=np.float32(0.125)),
                    inc="sse")
                q_toks.append(ev)
            else:
                dt = m - 8
                P.wait("vector", tok)
                P.wait("vector", st["kv_free"])
                s0 = (4 * p) % SLOTS
                run1 = min(SLOTS - s0, 4)
                ev = P.op("vector", lambda e, dt=dt, s0=s0, run1=run1, bank=bank:
                          e.tensor_copy(
                    kring[:, dt, s0:s0 + run1, :], pm[bank][:, 0:run1 * 128]), inc="sve")
                if run1 < 4:
                    ev = P.op("vector", lambda e, dt=dt, run1=run1, bank=bank:
                              e.tensor_copy(
                        kring[:, dt, 0:4 - run1, :], pm[bank][:, run1 * 128:512]),
                        inc="sve")
                k_toks.append(ev)
            st["pm_free"][bank] = q_toks[-1] if m < 8 else k_toks[-1]
            yield
        for tt in range(NQT):
            for half in range(2):
                w_tok, slot = slab(c, SL_V0 + half)
                bank = next_pm()
                P.wait("tensor", w_tok)
                P.wait("tensor", st["pm_free"][bank])
                for k in range(NT):
                    tok = P.op("tensor", lambda e, tt=tt, k=k, slot=slot, bank=bank,
                               par=par: e.matmul(
                        pm[bank][:, :], xn[:, par, k, tt * 128:(tt + 1) * 128],
                        w_v(slot, k), start=(k == 0), stop=(k == NT - 1)),
                        inc="spe" if k == NT - 1 else None)
                if tt == NQT - 1:
                    slab_done(c, SL_V0 + half, tok)
                P.wait("vector", tok)
                P.wait("vector", st["kv_free"])
                slot_v = (4 * p + tt) % SLOTS
                ev = P.op("vector", lambda e, slot_v=slot_v, half=half, bank=bank:
                          e.tensor_copy(
                    vring[:, slot_v, half * 512:(half + 1) * 512], pm[bank][:, :]),
                    inc="sve")
                v_toks.append(ev)
                st["pm_free"][bank] = ev
            yield
        kv_wr = k_toks + v_toks
        pair_ev = []
        pv_last = None
        for t in range(NQT):
            gt = 4 * p + t
            j0 = max(0, gt - 2)
            jt_list = list(range(j0, gt + 1))
            nj = len(jt_list)
            W = 128 * nj
            moff = {3: 0, 2: 384, 1: 640}[nj]
            for h in range(H):
                pp = h // 2
                P.waits("tensor", kv_wr + q_toks)
                P.wait("tensor", st["psc_free"])
                P.wait("tensor", t_consts)
                runs = []
                for jt in jt_list:
                    sl_ = jt % SLOTS
                    if runs and runs[-1][0] + runs[-1][1] == sl_:
                        runs[-1][1] += 1
                    else:
                        runs.append([sl_, 1])
                col = 0
                first = True
                for sl_, ln_ in runs:
                    P.op("tensor", lambda e, h=h, t=t, sl_=sl_, ln_=ln_, col=col,
                         first=first: e.matmul(
                        psc[:, col:col + ln_ * 128],
                        qb[64 * (h % 2):64 * (h % 2) + 64, h // 2, t * 128:(t + 1) * 128],
                        kring[64 * (h % 2):64 * (h % 2) + 64, h // 2, sl_:sl_ + ln_, :],
                        start=first, stop=False))
                    col += ln_ * 128
                    first = False
                t_scm = P.op("tensor", lambda e, W=W, moff=moff: e.matmul(
                    psc[:, 0:W], idm[:, :], msk[:, moff:moff + W],
                    start=False, stop=True), inc="spe")
                P.wait("vector", t_scm)
                t_mx = P.op("vector", lambda e, W=W: e.reduce_max(
                    nmax[:, :], psc[:, 0:W], axis=AX.X, negate=True), inc="sve")
                P.wait("scalar", t_mx)
                P.wait("scalar", st["attn_free"])
                t_ex = P.op("scalar", lambda e, W=W: e.activation(
                    attn[:, 0:W], psc[:, 0:W], AF.Exp,
                    bias=nmax[:, :], scale=np.float32(1.0), accum_out=esum[:, :]),
                    inc="sse")
                st["psc_free"] = t_ex
                P.wait("vector", t_ex)
                t_rc = P.op("vector", lambda e: e.reciprocal(rsum[:, :], esum[:, :]),
                            inc="sve")
                P.wait("scalar", t_rc)
                t_nm = P.op("scalar", lambda e, W=W: e.activation(
                    attn[:, 0:W], attn[:, 0:W], AF.Copy, scale=rsum[:, :]), inc="sse")
                yield
                P.wait("tensor", t_nm)
                P.wait("tensor", st["ptr_free"])
                for bi in range(nj):
                    t_tr = P.op("tensor", lambda e, bi=bi: e.transpose(
                        ptrb[:, bi, :], attn[:, bi * 128:(bi + 1) * 128], idm[:, :]),
                        inc="spe" if bi == nj - 1 else None)
                st["attn_free"] = t_tr
                P.wait("vector", t_tr)
                P.wait("vector", st["attnT_free"])
                t_cp = P.op("vector", lambda e, nj=nj: e.tensor_copy(
                    attnT[:, 0:nj, :], ptrb[:, 0:nj, :]), inc="sve")
                st["ptr_free"] = t_cp
                P.wait("tensor", t_cp)
                if h % 2 == 0:
                    P.wait("tensor", st["ppv_free"])
                for bi, jt in enumerate(jt_list):
                    t_pv = P.op("tensor", lambda e, h=h, bi=bi, jt=jt, nj=nj: e.matmul(
                        ppv[64 * (h % 2):64 * (h % 2) + 64, 0:128],
                        vring[:, jt % SLOTS, h * 64:(h + 1) * 64],
                        attnT[:, bi, :],
                        start=(bi == 0), stop=(bi == nj - 1)),
                        inc="spe" if (bi == nj - 1 and h % 2 == 1) else None)
                st["attnT_free"] = t_pv
                if h % 2 == 1:
                    pv_last = t_pv
                    P.wait("vector", t_pv)
                    ev = P.op("vector", lambda e, pp=pp, t=t: e.tensor_copy(
                        aout[:, pp, t * 128:(t + 1) * 128], ppv[:, 0:128]), inc="sve")
                    pair_ev.append(ev)
                    st["ppv_free"] = ev
                yield
        st["kv_free"] = pv_last
        st["q_free"] = pv_last
        st["front"][c] = {"pair_ev": pair_ev, "v_toks": v_toks, "rows_tok": t_all}

    def serial(c):
        par = c % 2
        fo = st["front"][c]
        pair_ev = fo["pair_ev"]
        t_r1B = bcast(0, R_R1, fo["rows_tok"])
        t_mrB = bcast(1, R_MR1, fo["rows_tok"])
        proj_toks = []
        for m in range(8):
            w_tok, slot = slab(c, SL_PROJ0 + m // 4)
            bank = next_pm()
            P.wait("tensor", w_tok)
            P.wait("tensor", st["pm_free"][bank])
            P.waits("tensor", pair_ev)
            for k in range(NT):
                tok = P.op("tensor", lambda e, m=m, k=k, slot=slot, bank=bank: e.matmul(
                    pm[bank][:, :], w_qk(slot, k, m % 4), aout[:, k, :],
                    start=(k == 0), stop=(k == NT - 1)),
                    inc="spe" if k == NT - 1 else None)
            if m % 4 == 3:
                slab_done(c, SL_PROJ0 + m // 4, tok)
            P.waits("vector", [tok, t_r1B, t_mrB])
            P.wait("vector", st["sq_free"][m % 2])
            P.wait("vector", st["abo_store"])
            P.op("vector", lambda e, m=m: e.tensor_mul(scr[:, m % 2, :], xb[:, m, :],
                                                       pbc[0][:, :]))
            P.op("vector", lambda e, m=m: e.tensor_sub(scr[:, m % 2, :],
                                                       scr[:, m % 2, :], pbc[1][:, :]))
            ev = P.op("vector", lambda e, m=m, bank=bank: e.tensor_add(
                abo[:, m, :], pm[bank][:, :], scr[:, m % 2, :]), inc="sve")
            st["sq_free"][m % 2] = ev
            proj_toks.append(ev)
            st["pm_free"][bank] = ev
        st["pbc_free"] = [proj_toks[-1], proj_toks[-1]]
        stats_tok = ln_stats(abo, proj_toks)
        t_mr = ln_rows(stats_tok, full=False)
        t_rB = bcast(0, R_T, t_mr)
        t_mrB = bcast(1, R_R, t_mr)
        x2_toks = []
        for t in range(NT):
            P.waits("vector", [t_rB, t_mrB])
            P.wait("vector", st["sq_free"][t % 2])
            P.wait("vector", st["x2_free"])
            P.op("vector", lambda e, t=t: e.tensor_mul(scr[:, t % 2, :], abo[:, t, :],
                                                       pbc[0][:, :]))
            P.op("vector", lambda e, t=t: e.tensor_sub(scr[:, t % 2, :],
                                                       scr[:, t % 2, :], pbc[1][:, :]))
            tok = P.op("vector", lambda e, t=t: e.tensor_add(
                x2b[:, t, :], xb[:, t, :], scr[:, t % 2, :]), inc="sve")
            st["sq_free"][t % 2] = tok
            x2_toks.append(tok)
        st["pbc_free"] = [x2_toks[-1], x2_toks[-1]]
        st["x_free"] = x2_toks[-1]
        stats_tok = ln_stats(x2b, x2_toks, bf=True)
        t_mr = ln_rows(stats_tok, full=False)
        t_rB = bcast(0, R_T, t_mr)
        t_mrB = bcast(1, R_R, t_mr)
        xn2_toks = []
        for t in range(NT):
            P.waits("vector", [t_rB, t_mrB])
            P.waits("vector", fo["v_toks"])
            P.wait("vector", st["xn_free"][par])
            P.wait("vector", st["sq_free"][t % 2])
            P.op("vector", lambda e, t=t: e.tensor_mul(scr[:, t % 2, :], x2b[:, t, :],
                                                       pbc[0][:, :]))
            tok = P.op("vector", lambda e, t=t, par=par: e.tensor_sub(
                xn[:, par, t, :], scr[:, t % 2, :], pbc[1][:, :]), inc="sve")
            st["sq_free"][t % 2] = tok
            xn2_toks.append(tok)
        st["pbc_free"] = [xn2_toks[-1], xn2_toks[-1]]
        st["front"][c]["xn2_toks"] = xn2_toks

    def ffn(c, gen):
        par = c % 2
        fo = st["front"][c]
        xn2_toks = fo["xn2_toks"]

        def g_next(n=1):
            if gen is None:
                return
            for _ in range(n):
                try:
                    next(gen)
                except StopIteration:
                    return

        silu_toks = [None] * 32
        for m in range(32):
            w_tok, slot = slab(c, SL_W10 + m // 4)
            bank = next_pm()
            P.wait("tensor", w_tok)
            P.wait("tensor", st["pm_free"][bank])
            P.waits("tensor", xn2_toks)
            for k in range(NT):
                tok = P.op("tensor", lambda e, m=m, k=k, slot=slot, bank=bank, par=par:
                           e.matmul(
                    pm[bank][:, :], w_qk(slot, k, m % 4), xn[:, par, k, :],
                    start=(k == 0), stop=(k == NT - 1)),
                    inc="spe" if k == NT - 1 else None)
            if m % 4 == 3:
                slab_done(c, SL_W10 + m // 4, tok)
            P.wait("scalar", tok)
            P.wait("scalar", st["h_free"])
            ev = P.op("scalar", lambda e, m=m, bank=bank: e.activation(
                hbuf[:, m, :], pm[bank][:, :], AF.Silu, scale=c_s1(m)), inc="sse")
            silu_toks[m] = ev
            st["pm_free"][bank] = ev
            g_next()
        h_toks = [None] * 32
        for m in range(32):
            w_tok, slot = slab(c, SL_W20 + m // 4)
            bank = next_pm()
            P.wait("tensor", w_tok)
            P.wait("tensor", st["pm_free"][bank])
            for k in range(NT):
                tok = P.op("tensor", lambda e, m=m, k=k, slot=slot, bank=bank, par=par:
                           e.matmul(
                    pm[bank][:, :], w_qk(slot, k, m % 4), xn[:, par, k, :],
                    start=(k == 0), stop=(k == NT - 1)),
                    inc="spe" if k == NT - 1 else None)
            if m % 4 == 3:
                slab_done(c, SL_W20 + m // 4, tok)
            P.wait("vector", tok)
            P.wait("vector", silu_toks[m])
            ev = P.op("vector", lambda e, m=m, bank=bank: e.scalar_tensor_tensor(
                hbuf[:, m, :], pm[bank][:, :], c_s2(m), hbuf[:, m, :],
                op0=ALU.mult, op1=ALU.mult), inc="sve")
            h_toks[m] = ev
            st["pm_free"][bank] = ev
            g_next()
        st["xn_free"][par] = h_toks[31]

        def big(sl0, src, src_toks, dst, dst_free, scale_col):
            toks = [None] * 32
            for m in range(32):
                w_tok, slot = slab(c, sl0 + m)
                bank = next_pm()
                P.wait("tensor", w_tok)
                P.wait("tensor", st["pm_free"][bank])
                P.waits("tensor", src_toks)
                for k in range(32):
                    tok = P.op("tensor", lambda e, m=m, k=k, slot=slot, bank=bank,
                               src=src: e.matmul(
                        pm[bank][:, :], w_h(slot, k), src[:, k, :],
                        start=(k == 0), stop=(k == 31)),
                        inc="spe" if k == 31 else None)
                slab_done(c, sl0 + m, tok)
                P.wait("scalar", tok)
                P.wait("scalar", dst_free)
                ev = P.op("scalar", lambda e, m=m, dst=dst, bank=bank: e.activation(
                    dst[:, m, :], pm[bank][:, :], AF.Gelu, scale=scale_col(m)),
                    inc="sse")
                toks[m] = ev
                st["pm_free"][bank] = ev
                g_next(2)
            return toks

        hp_toks = big(SL_H10, hbuf, h_toks, hpbuf, st["hp_free"], c_sh1)
        hpp_toks = big(SL_H20, hpbuf, hp_toks, hbuf, None, c_sh2)
        st["hp_free"] = hpp_toks[31]
        f_toks = []
        for m in range(8):
            w_tok, slot = slab(c, SL_W30 + m)
            bank = next_pm()
            P.wait("tensor", w_tok)
            P.wait("tensor", st["pm_free"][bank])
            P.waits("tensor", hpp_toks)
            for k in range(32):
                tok = P.op("tensor", lambda e, m=m, k=k, slot=slot, bank=bank: e.matmul(
                    pm[bank][:, :], w_h(slot, k), hbuf[:, k, :],
                    start=(k == 0), stop=(k == 31)),
                    inc="spe" if k == 31 else None)
            slab_done(c, SL_W30 + m, tok)
            P.wait("scalar", tok)
            ev = P.op("scalar", lambda e, m=m, bank=bank: e.activation(
                abo[:, m, :], pm[bank][:, :], AF.Copy, scale=c_s3(m)), inc="sse")
            f_toks.append(ev)
            st["pm_free"][bank] = ev
            g_next()
        st["h_free"] = f_toks[-1]
        b, p = divmod(c, CHUNKS_PER_SEQ)
        stats_tok = ln_stats(abo, f_toks)
        t_mr = ln_rows(stats_tok, full=False)
        t_rB = bcast(0, R_T, t_mr)
        t_mrB = bcast(1, R_R, t_mr)
        out_toks = []
        for t in range(NT):
            P.waits("vector", [t_rB, t_mrB])
            P.wait("vector", st["sq_free"][t % 2])
            P.op("vector", lambda e, t=t: e.tensor_mul(scr[:, t % 2, :], abo[:, t, :],
                                                       pbc[0][:, :]))
            P.op("vector", lambda e, t=t: e.tensor_sub(scr[:, t % 2, :],
                                                       scr[:, t % 2, :], pbc[1][:, :]))
            tok = P.op("vector", lambda e, t=t: e.tensor_add(
                abo[:, t, :], x2b[:, t, :], scr[:, t % 2, :]), inc="sve")
            st["sq_free"][t % 2] = tok
            out_toks.append(tok)
        st["pbc_free"] = [out_toks[-1], out_toks[-1]]
        st["x2_free"] = out_toks[-1]
        P.waits("sync", out_toks)
        st["abo_store"] = P.op("sync", lambda e, b=b, p=p: e.dma_start(
            out=out_t[b, :, p * T:(p + 1) * T].rearrange("(nt q) t -> q nt t", q=128),
            in_=abo[:, :, :]), inc="sst", k=16)
        if gen is not None:
            for _ in gen:
                pass
        del st["front"][c]

    g = front(0)
    for _ in g:
        pass
    for c in range(NCH):
        serial(c)
        gen = front(c + 1) if c + 1 < NCH else None
        ffn(c, gen)

    final = P.cnt["sst"]
    P.raw("sync", lambda e: e.wait_ge(sems["sst"], final))

    with nc.Block() as block:
        @block.sync
        def _(sync):
            for fn in P.ops["sync"]:
                fn(sync)

        @block.gpsimd
        def _(gpsimd):
            for fn in P.ops["gpsimd"]:
                fn(gpsimd)

        @block.tensor
        def _(tensor):
            for fn in P.ops["tensor"]:
                fn(tensor)

        @block.vector
        def _(vector):
            for fn in P.ops["vector"]:
                fn(vector)

        @block.scalar
        def _(scalar):
            for fn in P.ops["scalar"]:
                fn(scalar)
    ctx.close()
    return nc


def kernel(**inputs):
    x = np.asarray(inputs["L0_attn_x"])
    slabs, masks, ident, scales = _host_pack({k: np.asarray(v) for k, v in inputs.items()})
    nc = build_nc()
    in_maps = []
    for core in range(NCORES):
        xc = x[core * SEQ_PER_CORE:(core + 1) * SEQ_PER_CORE]
        x_cm = np.ascontiguousarray(xc.transpose(0, 2, 1))
        in_maps.append({"x": x_cm, "w": slabs, "mk": masks, "idin": ident, "sc": scales})
    res = run_bass_kernel_spmd(nc, in_maps, core_ids=list(range(NCORES)))
    outs = []
    for core in range(NCORES):
        o = np.asarray(res.results[core]["out"])
        outs.append(np.ascontiguousarray(o.transpose(0, 2, 1)))
    return np.concatenate(outs, axis=0).astype(np.float32)


# version 13
# speedup vs baseline: 1.1251x; 1.0253x over previous
import numpy as np
import ml_dtypes
import concourse.bass as bass
import concourse.mybir as mybir
from concourse.bass_utils import run_bass_kernel_spmd

BF = ml_dtypes.bfloat16
F32 = mybir.dt.float32
BF16 = mybir.dt.bfloat16
AF = mybir.ActivationFunctionType
ALU = mybir.AluOpType
AX = mybir.AxisListType

D = 1024
S = 2048
NB = 32
NCORES = 8
SEQ_PER_CORE = NB // NCORES
T = 512
NT = 8
NQT = 4
CHUNKS_PER_SEQ = S // T
H = 16
EPS = 1e-5
NEG = -1.0e30
SLOTS = 6
WSLOTS = 3
NSLAB = 96
SLAB_E = 4096

SL_QK0, SL_V0, SL_PROJ0, SL_W10, SL_W20, SL_H10, SL_H20, SL_W30 = 0, 4, 6, 8, 16, 24, 56, 88


def _pack_lhsT(WT, Kt, m_per_slab):
    K, M = WT.shape
    Mt = M // 128
    arr = WT.reshape(Kt, 128, Mt, 128)
    ns = Mt // m_per_slab
    arr = arr.reshape(Kt, 128, ns, m_per_slab, 128)
    return np.ascontiguousarray(arr.transpose(2, 1, 0, 3, 4))


def _host_pack(inputs):
    wqkv = inputs["L0_attn_W_qkv"]
    wqk_T = np.ascontiguousarray(wqkv[:2048].T)
    wv_T = np.ascontiguousarray(wqkv[2048:].T)
    wout_T = np.ascontiguousarray(inputs["L0_attn_W_out"].T)
    w1_T = np.ascontiguousarray(inputs["L0_ffn_w1_w"].T)
    w2_T = np.ascontiguousarray(inputs["L0_ffn_w2_w"].T)
    h1_T = np.ascontiguousarray(inputs["L0_ffn_h1_w"].T)
    h2_T = np.ascontiguousarray(inputs["L0_ffn_h2_w"].T)
    w3_T = np.ascontiguousarray(inputs["L0_ffn_w3_w"].T)

    slabs = np.empty((NSLAB, 128, SLAB_E), dtype=np.float32)
    slabs[SL_QK0:SL_QK0 + 4] = _pack_lhsT(wqk_T, 8, 4).reshape(4, 128, SLAB_E)
    vv = wv_T.reshape(8, 128, 1024)
    slabs[SL_V0] = vv[:, :, :512].transpose(1, 0, 2).reshape(128, SLAB_E)
    slabs[SL_V0 + 1] = vv[:, :, 512:].transpose(1, 0, 2).reshape(128, SLAB_E)
    slabs[SL_PROJ0:SL_PROJ0 + 2] = _pack_lhsT(wout_T, 8, 4).reshape(2, 128, SLAB_E)
    slabs[SL_W10:SL_W10 + 8] = _pack_lhsT(w1_T, 8, 4).reshape(8, 128, SLAB_E)
    slabs[SL_W20:SL_W20 + 8] = _pack_lhsT(w2_T, 8, 4).reshape(8, 128, SLAB_E)
    slabs[SL_H10:SL_H10 + 32] = _pack_lhsT(h1_T, 32, 1).reshape(32, 128, SLAB_E)
    slabs[SL_H20:SL_H20 + 32] = _pack_lhsT(h2_T, 32, 1).reshape(32, 128, SLAB_E)
    slabs[SL_W30:SL_W30 + 8] = _pack_lhsT(w3_T, 32, 1).reshape(8, 128, SLAB_E)

    ii = np.arange(128)[:, None]
    masks = np.full((128, 768), NEG, dtype=np.float32)
    j = np.arange(384)[None, :]
    masks[:, 0:384] = np.where((j >= ii) & (j <= ii + 256), 0.0, NEG)
    j = np.arange(256)[None, :]
    masks[:, 384:640] = np.where(j <= ii + 128, 0.0, NEG)
    j = np.arange(128)[None, :]
    masks[:, 640:768] = np.where(j <= ii, 0.0, NEG)

    ident = np.eye(128, dtype=np.float32)

    def col128(v):
        return np.ascontiguousarray(v.reshape(-1, 128).T)

    scales = np.concatenate([
        col128(inputs["L0_ffn_w1_s"]), col128(inputs["L0_ffn_w2_s"]),
        col128(inputs["L0_ffn_h1_s"]), col128(inputs["L0_ffn_h2_s"]),
        col128(inputs["L0_ffn_w3_s"]), col128(inputs["L0_attn_gamma"]),
        col128(inputs["L0_attn_beta"]),
    ], axis=1).astype(np.float32)

    return slabs, masks, ident, scales


class Tok:
    __slots__ = ("sem", "val", "eng")
    def __init__(self, sem, val, eng):
        self.sem, self.val, self.eng = sem, val, eng


class Plan:
    ENGINES = ("sync", "gpsimd", "tensor", "vector", "scalar")

    def __init__(self, sems):
        self.ops = {e: [] for e in self.ENGINES}
        self.sems = sems
        self.cnt = {name: 0 for name in sems}
        self.hwm = {e: {name: 0 for name in sems} for e in self.ENGINES}

    def wait(self, eng, tok):
        if tok is None or tok.eng == eng:
            return
        if self.hwm[eng][tok.sem] >= tok.val:
            return
        self.hwm[eng][tok.sem] = tok.val
        h, v = self.sems[tok.sem], tok.val
        self.ops[eng].append(lambda e: e.wait_ge(h, v))

    def waits(self, eng, toks):
        for t in toks:
            self.wait(eng, t)

    def op(self, eng, fn, inc=None, k=1):
        if inc is None:
            self.ops[eng].append(fn)
            return None
        self.cnt[inc] += k
        h = self.sems[inc]
        self.ops[eng].append(lambda e: fn(e).then_inc(h, k))
        return Tok(inc, self.cnt[inc], eng)

    def raw(self, eng, fn):
        self.ops[eng].append(fn)


DBG = False


def build_nc(nseq=SEQ_PER_CORE):
    NCH = nseq * CHUNKS_PER_SEQ
    nc = bass.Bass("TRN2", target_bir_lowering=False)
    x_in = nc.dram_tensor("x", [nseq, D, S], F32, kind="ExternalInput").ap()
    w_in = nc.dram_tensor("w", [NSLAB, 128, SLAB_E], F32, kind="ExternalInput").ap()
    mk_in = nc.dram_tensor("mk", [128, 768], F32, kind="ExternalInput").ap()
    id_in = nc.dram_tensor("idin", [128, 128], F32, kind="ExternalInput").ap()
    sc_in = nc.dram_tensor("sc", [128, 152], F32, kind="ExternalInput").ap()
    out_t = nc.dram_tensor("out", [nseq, D, S], F32, kind="ExternalOutput").ap()
    wscr = nc.dram_tensor("wscr", [NSLAB, 128, SLAB_E], BF16, kind="Internal").ap()

    import contextlib
    ctx = contextlib.ExitStack()
    sb = lambda name, shape, dt: ctx.enter_context(nc.sbuf_tensor(name, shape, dt))
    ps = lambda name, shape, dt: ctx.enter_context(nc.psum_tensor(name, shape, dt))

    xb = sb("xb", [128, NT, T], F32)
    x2b = sb("x2b", [128, NT, T], BF16)
    abo = sb("abo", [128, NT, T], F32)
    scr = sb("scr", [128, 2, T], F32)
    sqb = sb("sqb", [128, 2, T], BF16)
    xn = sb("xn", [128, 2, NT, T], BF16)
    qb = sb("qb", [128, NT, T], BF16)
    kring = sb("kring", [128, NT, SLOTS, 128], BF16)
    vring = sb("vring", [128, SLOTS, D], BF16)
    attn = sb("attn", [128, 384], BF16)
    attnT = sb("attnT", [128, 3, 128], BF16)
    aout = sb("aout", [128, NT, T], BF16)
    hbuf = sb("hbuf", [128, 32, T], BF16)
    hpbuf = sb("hpbuf", [128, 32, T], BF16)
    wring = sb("wring", [128, WSLOTS, SLAB_E], BF16)
    msk = sb("msk", [128, 768], BF16)
    idm = sb("idm", [128, 128], BF16)
    scsb = sb("scsb", [128, 152], F32)
    ones_c = sb("ones_c", [128, 1], F32)
    ones_b = sb("ones_b", [128, 1], BF16)
    ones_r = sb("ones_r", [1, 128], F32)
    rows = sb("rows", [1, 6, T], F32)
    nmax = sb("nmax", [128, 1], F32)
    esum = sb("esum", [128, 1], F32)
    rsum = sb("rsum", [128, 1], F32)

    R_M, R_VAR, R_T, R_R, R_R1, R_MR1 = 0, 1, 2, 3, 4, 5

    pm = [ps("pm0", [128, 512], F32), ps("pm1", [128, 512], F32)]
    pst = ps("pst", [128, 512], F32)
    pbc = [ps("pbc0", [128, 512], F32), ps("pbc1", [128, 512], F32)]
    psc = ps("psc", [128, 512], F32)
    ptrb = ps("ptrb", [128, 3, 128], BF16)
    ppv = ps("ppv", [128, 128], F32)

    sem_names = ["sx", "sw", "swf", "spro", "spe", "sve", "sse", "sst", "sgp"]
    sems = {n: ctx.enter_context(nc.semaphore(n)) for n in sem_names}
    P = Plan(sems)

    c_s1 = lambda m: scsb[:, m:m + 1]
    c_s2 = lambda m: scsb[:, 32 + m:33 + m]
    c_sh1 = lambda m: scsb[:, 64 + m:65 + m]
    c_sh2 = lambda m: scsb[:, 96 + m:97 + m]
    c_s3 = lambda m: scsb[:, 128 + m:129 + m]
    c_gam = lambda t: scsb[:, 136 + t:137 + t]
    c_bet = lambda t: scsb[:, 144 + t:145 + t]

    t_sc = P.op("sync", lambda e: e.dma_start(out=scsb[:, :], in_=sc_in[:, :]), inc="sx", k=16)
    P.op("gpsimd", lambda e: e.dma_start(out=msk[:, :], in_=mk_in[:, :]), inc="spro", k=16)
    t_consts = P.op("gpsimd", lambda e: e.dma_start(out=idm[:, :], in_=id_in[:, :]),
                    inc="spro", k=16)
    slab_cast = []
    for s_ in range(NSLAB):
        tok = P.op("gpsimd", lambda e, s_=s_: e.dma_start(out=wscr[s_], in_=w_in[s_]),
                   inc="spro", k=16)
        slab_cast.append(tok)
    P.op("gpsimd", lambda e: e.memset(ones_c[:, :], 1.0), inc="sgp")
    P.op("gpsimd", lambda e: e.memset(ones_b[:, :], 1.0), inc="sgp")
    t_ones = P.op("gpsimd", lambda e: e.memset(ones_r[:, :], 1.0), inc="sgp")

    slab_ready = {}
    slab_freed = {}
    gs_counter = [0]

    def load_slab(slab_idx):
        gs = gs_counter[0]
        gs_counter[0] += 1
        slot = gs % WSLOTS
        if gs >= WSLOTS:
            P.wait("sync", slab_freed[gs - WSLOTS])
        if gs < NSLAB:
            P.wait("sync", slab_cast[slab_idx])
        tok = P.op("sync", lambda e, slot=slot, si=slab_idx: e.dma_start(
            out=wring[:, slot, :], in_=wscr[si]), inc="sw", k=16)
        slab_ready[gs] = (tok, slot)
        return gs

    slab_gs = {}

    def slab(epoch, i):
        key = (epoch, i)
        if key not in slab_gs:
            slab_gs[key] = load_slab(i)
        tok, slot = slab_ready[slab_gs[key]]
        return tok, slot

    def slab_done(epoch, i, tok):
        slab_freed[slab_gs[(epoch, i)]] = tok

    def w_qk(slot, k, m):
        base = (k * 4 + m) * 128
        return wring[:, slot, base:base + 128]

    def w_v(slot, k):
        return wring[:, slot, k * 512:(k + 1) * 512]

    def w_h(slot, k):
        return wring[:, slot, k * 128:(k + 1) * 128]

    def row(i):
        return rows[:, i, :]

    st = {
        "sq_free": [None, None],
        "pm_free": [None, None],
        "pm_next": 0,
        "pbc_free": [None, None],
        "pst_free": None,
        "psc_free": None,
        "ptr_free": None,
        "ppv_free": None,
        "attn_free": None,
        "attnT_free": None,
        "last_bcast": [],
        "abo_store": None,
        "x_free": None,
        "x2_free": None,
        "hp_free": None,
        "h_free": None,
        "xn_free": [None, None],
        "q_free": None,
        "kv_free": None,
        "front": {},
    }

    def next_pm():
        b = st["pm_next"]
        st["pm_next"] ^= 1
        return b

    def ln_stats(src, src_toks, bf=False):
        ones = ones_b if bf else ones_c
        first = True
        mm_toks = []
        for t in range(NT):
            P.waits("vector", src_toks)
            if t >= 2:
                P.wait("vector", mm_toks[t - 2])
            elif st["sq_free"][t % 2] is not None:
                P.wait("vector", st["sq_free"][t % 2])
            tsq = P.op("vector", lambda e, t=t, src=src: e.tensor_mul(
                sqb[:, t % 2, :], src[:, t, :], src[:, t, :]), inc="sve")
            P.wait("tensor", tsq)
            if first:
                P.wait("tensor", st["pst_free"])
                P.wait("tensor", t_ones)
                first = False
            tok = P.op("tensor", lambda e, t=t: e.matmul(
                pst[32:33, :], ones_b[:, :], sqb[:, t % 2, :],
                start=(t == 0), stop=(t == NT - 1)), inc="spe")
            mm_toks.append(tok)
        for t in range(NT):
            P.waits("tensor", src_toks)
            tok = P.op("tensor", lambda e, t=t, ones=ones, src=src: e.matmul(
                pst[0:1, :], ones[:, :], src[:, t, :],
                start=(t == 0), stop=(t == NT - 1)), inc="spe")
        st["sq_free"] = [mm_toks[NT - 2], mm_toks[NT - 1]]
        return tok

    def ln_rows(stats_tok, full=False):
        P.waits("vector", st["last_bcast"])
        P.waits("scalar", st["last_bcast"])
        P.wait("vector", stats_tok)
        P.op("vector", lambda e: e.tensor_scalar_mul(row(R_M), pst[0:1, :], 1.0 / D))
        P.op("vector", lambda e: e.tensor_scalar_mul(row(R_T), pst[32:33, :], 1.0 / D))
        P.op("vector", lambda e: e.tensor_mul(row(R_VAR), row(R_M), row(R_M)))
        tv = P.op("vector", lambda e: e.tensor_sub(row(R_VAR), row(R_T), row(R_VAR)),
                  inc="sve")
        st["pst_free"] = tv
        P.wait("scalar", tv)
        if not full:
            P.op("scalar", lambda e: e.activation(row(R_T), row(R_VAR), AF.Ln,
                                                  bias=np.float32(EPS)))
            t_r = P.op("scalar", lambda e: e.activation(row(R_T), row(R_T), AF.Exp,
                                                        scale=np.float32(-0.5)), inc="sse")
            P.wait("vector", t_r)
            t_mr = P.op("vector", lambda e: e.tensor_mul(row(R_R), row(R_M), row(R_T)),
                        inc="sve")
            return t_mr
        P.op("scalar", lambda e: e.activation(row(R_T), row(R_VAR), AF.Ln,
                                              bias=np.float32(EPS)))
        t_r1 = P.op("scalar", lambda e: e.activation(row(R_R1), row(R_T), AF.Exp,
                                                     scale=np.float32(-0.5)), inc="sse")
        P.wait("vector", t_r1)
        t_vpe = P.op("vector", lambda e: e.tensor_scalar_add(row(R_T), row(R_VAR),
                                                             float(EPS)), inc="sve")
        P.op("vector", lambda e: e.reciprocal(row(R_T), row(R_T)))
        tvr = P.op("vector", lambda e: e.tensor_mul(row(R_T), row(R_VAR), row(R_T)),
                   inc="sve")
        P.wait("scalar", tvr)
        P.op("scalar", lambda e: e.activation(row(R_T), row(R_T), AF.Ln,
                                              bias=np.float32(EPS)))
        t_r2 = P.op("scalar", lambda e: e.activation(row(R_T), row(R_T), AF.Exp,
                                                     scale=np.float32(-0.5)), inc="sse")
        P.wait("vector", t_r2)
        P.op("vector", lambda e: e.tensor_mul(row(R_MR1), row(R_M), row(R_R1)))
        P.op("vector", lambda e: e.tensor_mul(row(R_R), row(R_R1), row(R_T)))
        t_all = P.op("vector", lambda e: e.tensor_mul(row(R_T), row(R_M), row(R_R)),
                     inc="sve")
        return t_all

    def bcast(bank, r_idx, dep_tok):
        P.wait("tensor", dep_tok)
        P.wait("tensor", st["pbc_free"][bank])
        tok = P.op("tensor", lambda e, bank=bank, r_idx=r_idx: e.matmul(
            pbc[bank][:, :], ones_r[:, :], row(r_idx), start=True, stop=True), inc="spe")
        st["last_bcast"].append(tok)
        st["last_bcast"] = st["last_bcast"][-4:]
        return tok

    def front(c):
        b, p = divmod(c, CHUNKS_PER_SEQ)
        par = c % 2
        if c == 0:
            for eng in ("tensor", "vector", "scalar"):
                P.wait(eng, slab_cast[NSLAB - 1])
            P.wait("scalar", t_sc)
            P.wait("vector", t_sc)
        P.wait("sync", st["x_free"])
        x_ready = P.op("sync", lambda e, b=b, p=p: e.dma_start(
            out=xb[:, :, :], in_=x_in[b, :, p * T:(p + 1) * T].rearrange(
                "(nt q) t -> q nt t", q=128)), inc="sx", k=16)
        yield
        stats_tok = ln_stats(xb, [x_ready])
        yield
        t_all = ln_rows(stats_tok, full=True)
        yield
        t_RB = bcast(0, R_R, t_all)
        t_MB = bcast(1, R_T, t_all)
        xn_toks = []
        for t in range(NT):
            P.waits("vector", [t_RB, t_MB, x_ready])
            P.wait("vector", st["sq_free"][t % 2])
            P.wait("vector", st["xn_free"][par])
            P.op("vector", lambda e, t=t: e.tensor_mul(scr[:, t % 2, :], xb[:, t, :],
                                                       pbc[0][:, :]))
            P.op("vector", lambda e, t=t: e.tensor_sub(scr[:, t % 2, :],
                                                       scr[:, t % 2, :], pbc[1][:, :]))
            tok = P.op("vector", lambda e, t=t, par=par: e.tensor_scalar(
                xn[:, par, t, :], scr[:, t % 2, :], c_gam(t), c_bet(t),
                op0=ALU.mult, op1=ALU.add), inc="sve")
            st["sq_free"][t % 2] = tok
            xn_toks.append(tok)
            if t % 2 == 1:
                yield
        st["pbc_free"] = [xn_toks[NT - 1], xn_toks[NT - 1]]
        q_toks, k_toks, v_toks = [], [], []
        for m in range(16):
            w_tok, slot = slab(c, SL_QK0 + m // 4)
            bank = next_pm()
            P.wait("tensor", w_tok)
            P.wait("tensor", st["pm_free"][bank])
            P.waits("tensor", xn_toks)
            for k in range(NT):
                tok = P.op("tensor", lambda e, m=m, k=k, slot=slot, bank=bank, par=par:
                           e.matmul(
                    pm[bank][:, :], w_qk(slot, k, m % 4), xn[:, par, k, :],
                    start=(k == 0), stop=(k == NT - 1)),
                    inc="spe" if k == NT - 1 else None)
            if m % 4 == 3:
                slab_done(c, SL_QK0 + m // 4, tok)
            if m < 8:
                P.wait("scalar", tok)
                P.wait("scalar", st["q_free"])
                ev = P.op("scalar", lambda e, m=m, bank=bank: e.activation(
                    qb[:, m, :], pm[bank][:, :], AF.Copy, scale=np.float32(0.125)),
                    inc="sse")
                q_toks.append(ev)
            else:
                dt = m - 8
                P.wait("vector", tok)
                P.wait("vector", st["kv_free"])
                s0 = (4 * p) % SLOTS
                run1 = min(SLOTS - s0, 4)
                ev = P.op("vector", lambda e, dt=dt, s0=s0, run1=run1, bank=bank:
                          e.tensor_copy(
                    kring[:, dt, s0:s0 + run1, :], pm[bank][:, 0:run1 * 128]), inc="sve")
                if run1 < 4:
                    ev = P.op("vector", lambda e, dt=dt, run1=run1, bank=bank:
                              e.tensor_copy(
                        kring[:, dt, 0:4 - run1, :], pm[bank][:, run1 * 128:512]),
                        inc="sve")
                k_toks.append(ev)
            st["pm_free"][bank] = q_toks[-1] if m < 8 else k_toks[-1]
            yield
        for tt in range(NQT):
            for half in range(2):
                w_tok, slot = slab(c, SL_V0 + half)
                bank = next_pm()
                P.wait("tensor", w_tok)
                P.wait("tensor", st["pm_free"][bank])
                for k in range(NT):
                    tok = P.op("tensor", lambda e, tt=tt, k=k, slot=slot, bank=bank,
                               par=par: e.matmul(
                        pm[bank][:, :], xn[:, par, k, tt * 128:(tt + 1) * 128],
                        w_v(slot, k), start=(k == 0), stop=(k == NT - 1)),
                        inc="spe" if k == NT - 1 else None)
                if tt == NQT - 1:
                    slab_done(c, SL_V0 + half, tok)
                P.wait("vector", tok)
                P.wait("vector", st["kv_free"])
                slot_v = (4 * p + tt) % SLOTS
                ev = P.op("vector", lambda e, slot_v=slot_v, half=half, bank=bank:
                          e.tensor_copy(
                    vring[:, slot_v, half * 512:(half + 1) * 512], pm[bank][:, :]),
                    inc="sve")
                v_toks.append(ev)
                st["pm_free"][bank] = ev
            yield
        kv_wr = k_toks + v_toks
        pair_ev = []
        pv_last = None
        for t in range(NQT):
            gt = 4 * p + t
            j0 = max(0, gt - 2)
            jt_list = list(range(j0, gt + 1))
            nj = len(jt_list)
            W = 128 * nj
            moff = {3: 0, 2: 384, 1: 640}[nj]
            for h in range(H):
                pp = h // 2
                P.waits("tensor", kv_wr + q_toks)
                P.wait("tensor", st["psc_free"])
                P.wait("tensor", t_consts)
                runs = []
                for jt in jt_list:
                    sl_ = jt % SLOTS
                    if runs and runs[-1][0] + runs[-1][1] == sl_:
                        runs[-1][1] += 1
                    else:
                        runs.append([sl_, 1])
                col = 0
                first = True
                for sl_, ln_ in runs:
                    P.op("tensor", lambda e, h=h, t=t, sl_=sl_, ln_=ln_, col=col,
                         first=first: e.matmul(
                        psc[:, col:col + ln_ * 128],
                        qb[64 * (h % 2):64 * (h % 2) + 64, h // 2, t * 128:(t + 1) * 128],
                        kring[64 * (h % 2):64 * (h % 2) + 64, h // 2, sl_:sl_ + ln_, :],
                        start=first, stop=False))
                    col += ln_ * 128
                    first = False
                t_scm = P.op("tensor", lambda e, W=W, moff=moff: e.matmul(
                    psc[:, 0:W], idm[:, :], msk[:, moff:moff + W],
                    start=False, stop=True), inc="spe")
                P.wait("vector", t_scm)
                t_mx = P.op("vector", lambda e, W=W: e.reduce_max(
                    nmax[:, :], psc[:, 0:W], axis=AX.X, negate=True), inc="sve")
                P.wait("scalar", t_mx)
                P.wait("scalar", st["attn_free"])
                t_ex = P.op("scalar", lambda e, W=W: e.activation(
                    attn[:, 0:W], psc[:, 0:W], AF.Exp,
                    bias=nmax[:, :], scale=np.float32(1.0), accum_out=esum[:, :]),
                    inc="sse")
                st["psc_free"] = t_ex
                P.wait("vector", t_ex)
                t_rc = P.op("vector", lambda e: e.reciprocal(rsum[:, :], esum[:, :]),
                            inc="sve")
                P.wait("scalar", t_rc)
                t_nm = P.op("scalar", lambda e, W=W: e.activation(
                    attn[:, 0:W], attn[:, 0:W], AF.Copy, scale=rsum[:, :]), inc="sse")
                yield
                P.wait("tensor", t_nm)
                P.wait("tensor", st["ptr_free"])
                for bi in range(nj):
                    t_tr = P.op("tensor", lambda e, bi=bi: e.transpose(
                        ptrb[:, bi, :], attn[:, bi * 128:(bi + 1) * 128], idm[:, :]),
                        inc="spe" if bi == nj - 1 else None)
                st["attn_free"] = t_tr
                P.wait("vector", t_tr)
                P.wait("vector", st["attnT_free"])
                t_cp = P.op("vector", lambda e, nj=nj: e.tensor_copy(
                    attnT[:, 0:nj, :], ptrb[:, 0:nj, :]), inc="sve")
                st["ptr_free"] = t_cp
                P.wait("tensor", t_cp)
                if h % 2 == 0:
                    P.wait("tensor", st["ppv_free"])
                for bi, jt in enumerate(jt_list):
                    t_pv = P.op("tensor", lambda e, h=h, bi=bi, jt=jt, nj=nj: e.matmul(
                        ppv[64 * (h % 2):64 * (h % 2) + 64, 0:128],
                        vring[:, jt % SLOTS, h * 64:(h + 1) * 64],
                        attnT[:, bi, :],
                        start=(bi == 0), stop=(bi == nj - 1)),
                        inc="spe" if (bi == nj - 1 and h % 2 == 1) else None)
                st["attnT_free"] = t_pv
                if h % 2 == 1:
                    pv_last = t_pv
                    P.wait("vector", t_pv)
                    ev = P.op("vector", lambda e, pp=pp, t=t: e.tensor_copy(
                        aout[:, pp, t * 128:(t + 1) * 128], ppv[:, 0:128]), inc="sve")
                    pair_ev.append(ev)
                    st["ppv_free"] = ev
                yield
        st["kv_free"] = pv_last
        st["q_free"] = pv_last
        st["front"][c] = {"pair_ev": pair_ev, "v_toks": v_toks, "rows_tok": t_all}

    def serial(c):
        par = c % 2
        fo = st["front"][c]
        pair_ev = fo["pair_ev"]
        t_r1B = bcast(0, R_R1, fo["rows_tok"])
        t_mrB = bcast(1, R_MR1, fo["rows_tok"])
        proj_toks = []
        for m in range(8):
            w_tok, slot = slab(c, SL_PROJ0 + m // 4)
            bank = next_pm()
            P.wait("tensor", w_tok)
            P.wait("tensor", st["pm_free"][bank])
            P.waits("tensor", pair_ev)
            for k in range(NT):
                tok = P.op("tensor", lambda e, m=m, k=k, slot=slot, bank=bank: e.matmul(
                    pm[bank][:, :], w_qk(slot, k, m % 4), aout[:, k, :],
                    start=(k == 0), stop=(k == NT - 1)),
                    inc="spe" if k == NT - 1 else None)
            if m % 4 == 3:
                slab_done(c, SL_PROJ0 + m // 4, tok)
            P.waits("vector", [tok, t_r1B, t_mrB])
            P.wait("vector", st["sq_free"][m % 2])
            P.wait("vector", st["abo_store"])
            P.op("vector", lambda e, m=m: e.tensor_mul(scr[:, m % 2, :], xb[:, m, :],
                                                       pbc[0][:, :]))
            P.op("vector", lambda e, m=m: e.tensor_sub(scr[:, m % 2, :],
                                                       scr[:, m % 2, :], pbc[1][:, :]))
            ev = P.op("vector", lambda e, m=m, bank=bank: e.tensor_add(
                abo[:, m, :], pm[bank][:, :], scr[:, m % 2, :]), inc="sve")
            st["sq_free"][m % 2] = ev
            proj_toks.append(ev)
            st["pm_free"][bank] = ev
        st["pbc_free"] = [proj_toks[-1], proj_toks[-1]]
        stats_tok = ln_stats(abo, proj_toks)
        t_mr = ln_rows(stats_tok, full=False)
        t_rB = bcast(0, R_T, t_mr)
        t_mrB = bcast(1, R_R, t_mr)
        x2_toks = []
        for t in range(NT):
            P.waits("vector", [t_rB, t_mrB])
            P.wait("vector", st["sq_free"][t % 2])
            P.wait("vector", st["x2_free"])
            P.op("vector", lambda e, t=t: e.tensor_mul(scr[:, t % 2, :], abo[:, t, :],
                                                       pbc[0][:, :]))
            P.op("vector", lambda e, t=t: e.tensor_sub(scr[:, t % 2, :],
                                                       scr[:, t % 2, :], pbc[1][:, :]))
            tok = P.op("vector", lambda e, t=t: e.tensor_add(
                x2b[:, t, :], xb[:, t, :], scr[:, t % 2, :]), inc="sve")
            st["sq_free"][t % 2] = tok
            x2_toks.append(tok)
        st["pbc_free"] = [x2_toks[-1], x2_toks[-1]]
        st["x_free"] = x2_toks[-1]
        stats_tok = ln_stats(x2b, x2_toks, bf=True)
        t_mr = ln_rows(stats_tok, full=False)
        t_rB = bcast(0, R_T, t_mr)
        t_mrB = bcast(1, R_R, t_mr)
        xn2_toks = []
        for t in range(NT):
            P.waits("vector", [t_rB, t_mrB])
            P.waits("vector", fo["v_toks"])
            P.wait("vector", st["xn_free"][par])
            P.wait("vector", st["sq_free"][t % 2])
            P.op("vector", lambda e, t=t: e.tensor_mul(scr[:, t % 2, :], x2b[:, t, :],
                                                       pbc[0][:, :]))
            tok = P.op("vector", lambda e, t=t, par=par: e.tensor_sub(
                xn[:, par, t, :], scr[:, t % 2, :], pbc[1][:, :]), inc="sve")
            st["sq_free"][t % 2] = tok
            xn2_toks.append(tok)
        st["pbc_free"] = [xn2_toks[-1], xn2_toks[-1]]
        st["front"][c]["xn2_toks"] = xn2_toks

    def ffn(c, gen):
        par = c % 2
        fo = st["front"][c]
        xn2_toks = fo["xn2_toks"]

        def g_next(n=1):
            if gen is None:
                return
            for _ in range(n):
                try:
                    next(gen)
                except StopIteration:
                    return

        silu_toks = [None] * 32
        for m in range(32):
            w_tok, slot = slab(c, SL_W10 + m // 4)
            bank = next_pm()
            P.wait("tensor", w_tok)
            P.wait("tensor", st["pm_free"][bank])
            P.waits("tensor", xn2_toks)
            for k in range(NT):
                tok = P.op("tensor", lambda e, m=m, k=k, slot=slot, bank=bank, par=par:
                           e.matmul(
                    pm[bank][:, :], w_qk(slot, k, m % 4), xn[:, par, k, :],
                    start=(k == 0), stop=(k == NT - 1)),
                    inc="spe" if k == NT - 1 else None)
            if m % 4 == 3:
                slab_done(c, SL_W10 + m // 4, tok)
            P.wait("scalar", tok)
            P.wait("scalar", st["h_free"])
            ev = P.op("scalar", lambda e, m=m, bank=bank: e.activation(
                hbuf[:, m, :], pm[bank][:, :], AF.Silu, scale=c_s1(m)), inc="sse")
            silu_toks[m] = ev
            st["pm_free"][bank] = ev
            g_next()
        h_toks = [None] * 32
        for m in range(32):
            w_tok, slot = slab(c, SL_W20 + m // 4)
            bank = next_pm()
            P.wait("tensor", w_tok)
            P.wait("tensor", st["pm_free"][bank])
            for k in range(NT):
                tok = P.op("tensor", lambda e, m=m, k=k, slot=slot, bank=bank, par=par:
                           e.matmul(
                    pm[bank][:, :], w_qk(slot, k, m % 4), xn[:, par, k, :],
                    start=(k == 0), stop=(k == NT - 1)),
                    inc="spe" if k == NT - 1 else None)
            if m % 4 == 3:
                slab_done(c, SL_W20 + m // 4, tok)
            P.wait("vector", tok)
            P.wait("vector", silu_toks[m])
            ev = P.op("vector", lambda e, m=m, bank=bank: e.scalar_tensor_tensor(
                hbuf[:, m, :], pm[bank][:, :], c_s2(m), hbuf[:, m, :],
                op0=ALU.mult, op1=ALU.mult), inc="sve")
            h_toks[m] = ev
            st["pm_free"][bank] = ev
            g_next()
        st["xn_free"][par] = h_toks[31]

        def big(sl0, src, src_toks, dst, dst_free, scale_col):
            toks = [None] * 32
            for m in range(32):
                w_tok, slot = slab(c, sl0 + m)
                bank = next_pm()
                P.wait("tensor", w_tok)
                P.wait("tensor", st["pm_free"][bank])
                P.waits("tensor", src_toks)
                for k in range(32):
                    tok = P.op("tensor", lambda e, m=m, k=k, slot=slot, bank=bank,
                               src=src: e.matmul(
                        pm[bank][:, :], w_h(slot, k), src[:, k, :],
                        start=(k == 0), stop=(k == 31)),
                        inc="spe" if k == 31 else None)
                slab_done(c, sl0 + m, tok)
                P.wait("scalar", tok)
                P.wait("scalar", dst_free)
                ev = P.op("scalar", lambda e, m=m, dst=dst, bank=bank: e.activation(
                    dst[:, m, :], pm[bank][:, :], AF.Gelu, scale=scale_col(m)),
                    inc="sse")
                toks[m] = ev
                st["pm_free"][bank] = ev
                g_next(2)
            return toks

        hp_toks = big(SL_H10, hbuf, h_toks, hpbuf, st["hp_free"], c_sh1)
        hpp_toks = big(SL_H20, hpbuf, hp_toks, hbuf, None, c_sh2)
        st["hp_free"] = hpp_toks[31]
        f_toks = []
        for m in range(8):
            w_tok, slot = slab(c, SL_W30 + m)
            bank = next_pm()
            P.wait("tensor", w_tok)
            P.wait("tensor", st["pm_free"][bank])
            P.waits("tensor", hpp_toks)
            for k in range(32):
                tok = P.op("tensor", lambda e, m=m, k=k, slot=slot, bank=bank: e.matmul(
                    pm[bank][:, :], w_h(slot, k), hbuf[:, k, :],
                    start=(k == 0), stop=(k == 31)),
                    inc="spe" if k == 31 else None)
            slab_done(c, SL_W30 + m, tok)
            P.wait("scalar", tok)
            ev = P.op("scalar", lambda e, m=m, bank=bank: e.activation(
                abo[:, m, :], pm[bank][:, :], AF.Copy, scale=c_s3(m)), inc="sse")
            f_toks.append(ev)
            st["pm_free"][bank] = ev
            g_next()
        st["h_free"] = f_toks[-1]
        if gen is not None:
            for _ in gen:
                pass
        b, p = divmod(c, CHUNKS_PER_SEQ)
        stats_tok = ln_stats(abo, f_toks)
        t_mr = ln_rows(stats_tok, full=False)
        t_rB = bcast(0, R_T, t_mr)
        t_mrB = bcast(1, R_R, t_mr)
        out_toks = []
        for t in range(NT):
            P.waits("vector", [t_rB, t_mrB])
            P.wait("vector", st["sq_free"][t % 2])
            P.op("vector", lambda e, t=t: e.tensor_mul(scr[:, t % 2, :], abo[:, t, :],
                                                       pbc[0][:, :]))
            P.op("vector", lambda e, t=t: e.tensor_sub(scr[:, t % 2, :],
                                                       scr[:, t % 2, :], pbc[1][:, :]))
            tok = P.op("vector", lambda e, t=t: e.tensor_add(
                abo[:, t, :], x2b[:, t, :], scr[:, t % 2, :]), inc="sve")
            st["sq_free"][t % 2] = tok
            out_toks.append(tok)
        st["pbc_free"] = [out_toks[-1], out_toks[-1]]
        st["x2_free"] = out_toks[-1]
        P.waits("sync", out_toks)
        st["abo_store"] = P.op("sync", lambda e, b=b, p=p: e.dma_start(
            out=out_t[b, :, p * T:(p + 1) * T].rearrange("(nt q) t -> q nt t", q=128),
            in_=abo[:, :, :]), inc="sst", k=16)
        del st["front"][c]

    g = front(0)
    for _ in g:
        pass
    for c in range(NCH):
        serial(c)
        gen = front(c + 1) if c + 1 < NCH else None
        ffn(c, gen)

    final = P.cnt["sst"]
    P.raw("sync", lambda e: e.wait_ge(sems["sst"], final))

    with nc.Block() as block:
        @block.sync
        def _(sync):
            for fn in P.ops["sync"]:
                fn(sync)

        @block.gpsimd
        def _(gpsimd):
            for fn in P.ops["gpsimd"]:
                fn(gpsimd)

        @block.tensor
        def _(tensor):
            for fn in P.ops["tensor"]:
                fn(tensor)

        @block.vector
        def _(vector):
            for fn in P.ops["vector"]:
                fn(vector)

        @block.scalar
        def _(scalar):
            for fn in P.ops["scalar"]:
                fn(scalar)
    ctx.close()
    return nc


def kernel(**inputs):
    x = np.asarray(inputs["L0_attn_x"])
    slabs, masks, ident, scales = _host_pack({k: np.asarray(v) for k, v in inputs.items()})
    nc = build_nc()
    in_maps = []
    for core in range(NCORES):
        xc = x[core * SEQ_PER_CORE:(core + 1) * SEQ_PER_CORE]
        x_cm = np.ascontiguousarray(xc.transpose(0, 2, 1))
        in_maps.append({"x": x_cm, "w": slabs, "mk": masks, "idin": ident, "sc": scales})
    res = run_bass_kernel_spmd(nc, in_maps, core_ids=list(range(NCORES)))
    outs = []
    for core in range(NCORES):
        o = np.asarray(res.results[core]["out"])
        outs.append(np.ascontiguousarray(o.transpose(0, 2, 1)))
    return np.concatenate(outs, axis=0).astype(np.float32)
